# version 04
# baseline (speedup 1.0000x reference)
import numpy as np
import ml_dtypes
import concourse.bass as bass
import concourse.mybir as mybir
from concourse.bass_utils import run_bass_kernel_spmd
from contextlib import ExitStack

F32 = mybir.dt.float32
BF16 = mybir.dt.bfloat16
AF = mybir.ActivationFunctionType
ALU = mybir.AluOpType
AXX = mybir.AxisListType.X

D = 2048
T = 2048
H = 16
NT = 16
ND = 16
NCH = 16
SCALE = 128.0 ** -0.5
EPS_ATT = 1e-6
EPS_LN = 1e-5
DEBUG = False
import os
PH = int(os.environ.get("KPH", "9"))
KSUB = int(os.environ.get("KSUB", "99"))
KATT = int(os.environ.get("KATT", "16"))
KSB = int(os.environ.get("KSB", "99"))


def carve(nc, region, offset_bytes, name, shape, dtype):
    per_part = int(np.prod(shape[1:])) * mybir.dt.size(dtype)
    base = nc.lookup_mloc(region).addr
    assert offset_bytes % 32 == 0
    hnd = nc.alloc_sbuf_tensor_at(name, list(shape), dtype,
                                  offset=base + offset_bytes)
    return hnd, offset_bytes + ((per_part + 31) // 32) * 32


class Stream:

    def __init__(self, sems, step=16):
        self.sems = sems
        self.cnt = [0] * len(sems)
        self.i = 0
        self.step = step

    def next_slot(self):
        k = self.i % len(self.sems)
        self.i += 1
        prior = self.cnt[k]
        self.cnt[k] += self.step
        return self.sems[k], prior, self.cnt[k]

    def val(self):
        return tuple((s, c) for s, c in zip(self.sems, self.cnt) if c > 0)


def _flat(toks):
    out = []
    for t in toks:
        if t is None:
            continue
        if isinstance(t, tuple) and len(t) == 2 and not isinstance(t[0], tuple):
            out.append(t)
        else:
            out.extend(x for x in t if x is not None)
    return tuple(t for t in out if t[1] > 0)


class B:
    ENGS = ("sync", "scalar", "vector", "tensor", "gpsimd")

    def __init__(self):
        self.ops = {k: [] for k in self.ENGS}
        self.cnt = {}

    def _inc(self, sem, n):
        self.cnt[sem.name] = self.cnt.get(sem.name, 0) + n
        return (sem, self.cnt[sem.name])

    def val(self, sem):
        return (sem, self.cnt.get(sem.name, 0))

    def em(self, engine, fn):
        self.ops[engine].append(fn)

    def dma(self, engine, out_ap, in_ap, stream, transpose=False):
        sem, prior, newv = stream.next_slot()
        self.em(engine, lambda e, o=out_ap, i=in_ap, s=sem, p=prior,
                tr=transpose: (
            e.wait_ge(s, p) if p > 0 else None,
            (e.dma_start_transpose(out=o, in_=i) if tr
             else e.dma_start(out=o, in_=i)).then_inc(s, 16)))
        return (sem, newv)

    def xbar(self, engine, out_ap, in_ap, stream):
        return self.dma(engine, out_ap, in_ap, stream, transpose=True)

    def wait(self, engine, *toks):
        toks = _flat(toks)
        if toks:
            self.em(engine, lambda e, tk=toks: [e.wait_ge(s, v) for s, v in tk])

    def run(self, engine, fn, sem):
        prev = self.cnt.get(sem.name, 0)
        t = self._inc(sem, 1)
        if engine in ("vector", "scalar", "gpsimd"):
            self.em(engine, lambda e, f=fn, s=sem, p=prev: (
                e.wait_ge(s, p) if p > 0 else None, f(e).then_inc(s, 1)))
        else:
            self.em(engine, lambda e, f=fn, s=sem: f(e).then_inc(s, 1))
        return t

    def mark(self, engine, sem):
        t = self._inc(sem, 1)
        self.em(engine, lambda e, s=sem: e.sem_inc(s, 1))
        return t


def build_program():
    nc = bass.Bass("TRN2", target_bir_lowering=False)

    x0 = nc.dram_tensor("x0", [T, D], F32, kind="ExternalInput").ap()
    w_in = {n: nc.dram_tensor(f"w_{n}", [D, D], F32, kind="ExternalInput").ap()
            for n in "qkvo"}
    mems = nc.dram_tensor("mems", [4, H, 128, 128], F32, kind="ExternalInput").ap()
    mn_in = nc.dram_tensor("mn", [H * 4, 128], F32, kind="ExternalInput").ap()
    gate_in = nc.dram_tensor("gate", [1, H], F32, kind="ExternalInput").ap()
    rw_in = nc.dram_tensor("r_w", [64, D], F32, kind="ExternalInput").ap()
    rb_in = nc.dram_tensor("r_b", [1, 64], F32, kind="ExternalInput").ap()
    eb_in = nc.dram_tensor("eb", [64, D], F32, kind="ExternalInput").ap()
    identc = nc.dram_tensor("identc", [128, 128], BF16, kind="ExternalInput").ap()
    maskTc = nc.dram_tensor("maskTc", [128, 128], BF16, kind="ExternalInput").ap()
    mgrpc = nc.dram_tensor("mgrpc", [64, 4], BF16, kind="ExternalInput").ap()
    ones1c = nc.dram_tensor("ones1c", [1, 128], BF16, kind="ExternalInput").ap()
    sflagc = nc.dram_tensor("sflagc", [128, 1], F32, kind="ExternalInput").ap()
    rflagc = nc.dram_tensor("rflagc", [128, 1], F32, kind="ExternalInput").ap()

    y_out = nc.dram_tensor("y", [T, D], F32, kind="ExternalOutput").ap()
    if DEBUG:
        dbg_x1 = nc.dram_tensor("dbg_x1", [T, D], F32, kind="ExternalOutput").ap()
        dbg_x2 = nc.dram_tensor("dbg_x2", [T, D], F32, kind="ExternalOutput").ap()
        dbg_st = nc.dram_tensor("dbg_st", [128, 128], F32, kind="ExternalOutput").ap()

    wbf = {n: nc.dram_tensor(f"wbf_{n}", [D, D], BF16).ap() for n in "qkvo"}
    xln_tm = [nc.dram_tensor(f"xln{i}_tm", [T, D], BF16).ap() for i in range(3)]
    kp_hbm = nc.dram_tensor("kp_hbm", [H, T, 128], BF16).ap()
    v_hbm = nc.dram_tensor("v_hbm", [H, T, 128], BF16).ap()
    out_hbm = nc.dram_tensor("out_hbm", [H, T, 128], BF16).ap()
    x1_hbm = nc.dram_tensor("x1_hbm", [T, D], F32).ap()
    x2_hbm = nc.dram_tensor("x2_hbm", [T, D], F32).ap()
    cc_in = [nc.dram_tensor(f"cc_in{h}", [128, 136], F32) for h in range(H)]
    cc_out = [nc.dram_tensor(f"cc_out{h}", [128, 136], F32) for h in range(H)]

    with ExitStack() as ctx:
        E = ctx.enter_context
        bigA = [E(nc.sbuf_tensor(f"bigA{i}", [128, D], BF16)) for i in range(16)]
        bigB = [E(nc.sbuf_tensor(f"bigB{i}", [128, D], BF16)) for i in range(16)]
        R2 = E(nc.sbuf_tensor("R2", [128, 54 * 1024], mybir.dt.uint8))

        off = 0
        xt0, off = carve(nc, R2, off, "xt0", [128, D], F32)
        xt1, off = carve(nc, R2, off, "xt1", [128, D], F32)
        sqd, off = carve(nc, R2, off, "sqd", [128, D], F32)
        xlnst0, off = carve(nc, R2, off, "xlnst0", [128, D], BF16)
        xlnst1, off = carve(nc, R2, off, "xlnst1", [128, D], BF16)
        xlnst = [xlnst0, xlnst1]
        famF_end = off

        offP = 0
        qsb = []
        for i in range(2):
            t_, offP = carve(nc, R2, offP, f"qsb{i}", [128, 512], F32)
            qsb.append(t_)
        etm = []
        for i in range(2):
            t_, offP = carve(nc, R2, offP, f"etm{i}", [128, 1024], F32)
            etm.append(t_)
        kvst = []
        for i in range(2):
            t_, offP = carve(nc, R2, offP, f"kvst{i}", [128, 1024], BF16)
            kvst.append(t_)
        wT = []
        for i in range(16):
            t_, offP = carve(nc, R2, offP, f"wT{i}", [128, 1024], BF16)
            wT.append(t_)
        famP_end = offP

        offA = 0
        Lb = []
        for i in range(2):
            t_, offA = carve(nc, R2, offA, f"Lb{i}", [128, NCH * 129], BF16)
            Lb.append(t_)
        kpsd = []
        for i in range(2):
            t_, offA = carve(nc, R2, offA, f"kpsd{i}", [128, D], BF16)
            kpsd.append(t_)
        vaug = []
        for i in range(2):
            t_, offA = carve(nc, R2, offA, f"vaug{i}", [128, NCH * 132], BF16)
            vaug.append(t_)
        kpds = []
        for i in range(2):
            t_, offA = carve(nc, R2, offA, f"kpds{i}", [128, D], BF16)
            kpds.append(t_)
        statsS, outstS, steffS, scbS, ummS = [], [], [], [], []
        for i in range(2):
            t_, offA = carve(nc, R2, offA, f"stats{i}", [128, NCH * 129], BF16)
            statsS.append(t_)
            t_, offA = carve(nc, R2, offA, f"outst{i}", [128, D], BF16)
            outstS.append(t_)
            t_, offA = carve(nc, R2, offA, f"steff{i}", [128, 129], F32)
            steffS.append(t_)
            t_, offA = carve(nc, R2, offA, f"scb{i}", [128, 128], BF16)
            scbS.append(t_)
            t_, offA = carve(nc, R2, offA, f"umm{i}", [128, 2 * 128], F32)
            ummS.append(t_)
        famA_end = offA
        assert max(famP_end, famA_end) <= 54 * 1024, (famP_end, famA_end)

        ident = E(nc.sbuf_tensor("ident", [128, 128], BF16))
        maskT = E(nc.sbuf_tensor("maskT", [128, 128], BF16))
        ones1 = E(nc.sbuf_tensor("ones1", [1, 128], BF16))
        mgrp = E(nc.sbuf_tensor("mgrp", [64, 4], BF16))
        sflag = E(nc.sbuf_tensor("sflag", [128, 1], F32))
        rflag = E(nc.sbuf_tensor("rflag", [128, 1], F32))
        rwT = E(nc.sbuf_tensor("rwT", [128, 16 * 64], BF16))
        rw_bf = E(nc.sbuf_tensor("rw_bf", [64, D], BF16))
        eb_bf = E(nc.sbuf_tensor("eb_bf", [64, D], BF16))
        mn32 = E(nc.sbuf_tensor("mn32", [64, 128], F32))
        mn_bf = E(nc.sbuf_tensor("mn_bf", [64, 128], BF16))
        mnT = E(nc.sbuf_tensor("mnT", [128, 64], BF16))
        rbrow = E(nc.sbuf_tensor("rbrow", [1, 64], F32))
        rbrow_bf = E(nc.sbuf_tensor("rbrow_bf", [1, 64], BF16))
        rb_bc = E(nc.sbuf_tensor("rb_bc", [128, 64], F32))
        grow = E(nc.sbuf_tensor("grow", [1, 16], F32))
        grow_bf = E(nc.sbuf_tensor("grow_bf", [1, 16], BF16))
        g_bc = E(nc.sbuf_tensor("g_bc", [128, 16], F32))
        gm1_bc = E(nc.sbuf_tensor("gm1_bc", [128, 16], F32))
        erow = E(nc.sbuf_tensor("erow", [1, 16], F32))
        eU = E(nc.sbuf_tensor("eU", [1, 4], F32))
        rows_bf = E(nc.sbuf_tensor("rows_bf", [64, 1], BF16))
        relrow = E(nc.sbuf_tensor("relrow", [1, 64], F32))
        wgS = [E(nc.sbuf_tensor(f"wg{i}", [1, 12], F32)) for i in range(2)]
        wg_bfS = [E(nc.sbuf_tensor(f"wg_bf{i}", [1, 4], BF16)) for i in range(2)]
        wbcS = [E(nc.sbuf_tensor(f"wbc{i}", [128, 4], F32)) for i in range(2)]
        memh = [E(nc.sbuf_tensor(f"memh{i}", [128, 512], BF16)) for i in range(2)]
        qsum4 = E(nc.sbuf_tensor("qsum4", [128, 64], F32))
        qsum_bf = E(nc.sbuf_tensor("qsum_bf", [128, 16], BF16))
        qsum_f = E(nc.sbuf_tensor("qsum_f", [128, 16], F32))
        ccsb = E(nc.sbuf_tensor("ccsb", [128, 136], F32))
        ccrvS = [E(nc.sbuf_tensor(f"ccrv{i}", [128, 136], F32)) for i in range(2)]
        mcol = E(nc.sbuf_tensor("mcol", [128, 16], F32))
        rstd0 = E(nc.sbuf_tensor("rstd0", [128, 16], F32))
        negmr0 = E(nc.sbuf_tensor("negmr0", [128, 16], F32))
        rstd1 = E(nc.sbuf_tensor("rstd1", [128, 2], F32))
        negmr1 = E(nc.sbuf_tensor("negmr1", [128, 2], F32))
        rstd2 = E(nc.sbuf_tensor("rstd2", [128, 16], F32))
        negmr2 = E(nc.sbuf_tensor("negmr2", [128, 16], F32))
        sumc = E(nc.sbuf_tensor("sumc", [128, 48], F32))
        sqc = E(nc.sbuf_tensor("sqc", [128, 48], F32))
        colt = E(nc.sbuf_tensor("colt", [128, 8], F32))
        lncol = E(nc.sbuf_tensor("lncol", [128, 96], F32))
        swtc = E(nc.sbuf_tensor("swtc", [128, 16], F32))
        top8 = E(nc.sbuf_tensor("top8", [128, 8], F32))
        lgsb = E(nc.sbuf_tensor("lgsb", [128, 64], F32))
        esb = E(nc.sbuf_tensor("esb", [128, 64], F32))
        e8sb = E(nc.sbuf_tensor("e8sb", [128, 8], F32))
        mskb = E(nc.sbuf_tensor("mskb", [128, 64], F32))
        msbf = E(nc.sbuf_tensor("msbf", [128, 64], BF16))
        msT = E(nc.sbuf_tensor("msT", [64, 16 * 128], BF16))
        dcolS = [E(nc.sbuf_tensor(f"dcol{i}", [128, 4], F32)) for i in range(2)]
        ncolS = [E(nc.sbuf_tensor(f"ncol{i}", [128, 8], F32)) for i in range(2)]

        P = [E(nc.psum_tensor(f"P{i}", [128, 512], F32)) for i in range(7)]
        PT = E(nc.psum_tensor("PT", [128, 1024], BF16))

        SY = Stream([E(nc.semaphore(f"sy{i}")) for i in range(6)])
        SC = Stream([E(nc.semaphore(f"sc{i}")) for i in range(6)])
        GW = Stream([E(nc.semaphore(f"gw{i}")) for i in range(4)])
        GP = Stream([E(nc.semaphore(f"gp{i}")) for i in range(4)])
        CC = Stream([E(nc.semaphore(f"cc{i}")) for i in range(int(os.environ.get("KCC", "4")))], step=1)
        FIN = Stream([E(nc.semaphore(f"fin{i}")) for i in range(3)])
        s_pe = E(nc.semaphore("s_pe"))
        s_dv = E(nc.semaphore("s_dv"))
        s_ac = E(nc.semaphore("s_ac"))

        b = B()

        def barrier():
            waits = (b.val(s_pe), b.val(s_dv), b.val(s_ac), SY.val(),
                     SC.val(), GW.val(), GP.val())
            for eng in B.ENGS:
                b.wait(eng, *waits)

        def ln_stats(xt, rstd_a, negmr_a, ci, scol, dep):
            b.wait("vector", dep)
            b.run("vector", lambda e: e.reduce_sum(
                sumc[:, scol:scol + 1], xt[:, :], axis=AXX), s_dv)
            b.wait("scalar", dep)
            t2 = b.run("scalar", lambda e: e.activation(
                sqd[:, :], xt[:, :], AF.Square, accum_out=sqc[:, scol:scol + 1]),
                s_ac)
            b.wait("vector", t2)
            b.run("vector", lambda e: e.tensor_scalar(
                mcol[:, ci:ci + 1], sumc[:, scol:scol + 1], 1.0 / D, None,
                ALU.mult), s_dv)
            b.run("vector", lambda e: e.tensor_scalar(
                colt[:, 0:1], sqc[:, scol:scol + 1], 1.0 / D, None, ALU.mult),
                s_dv)
            b.run("vector", lambda e: e.tensor_scalar(
                colt[:, 1:2], mcol[:, ci:ci + 1], mcol[:, ci:ci + 1], None,
                ALU.mult), s_dv)
            t3 = b.run("vector", lambda e: e.tensor_scalar(
                colt[:, 0:1], colt[:, 0:1], colt[:, 1:2], EPS_LN,
                ALU.subtract, ALU.add), s_dv)
            b.wait("scalar", t3)
            b.run("scalar", lambda e: e.activation(colt[:, 2:3], colt[:, 0:1],
                                                   AF.Ln), s_ac)
            t5 = b.run("scalar", lambda e: e.activation(
                rstd_a[:, ci:ci + 1], colt[:, 2:3], AF.Exp, scale=-0.5), s_ac)
            b.wait("vector", t5)
            return b.run("vector", lambda e: e.tensor_scalar(
                negmr_a[:, ci:ci + 1], mcol[:, ci:ci + 1], rstd_a[:, ci:ci + 1],
                -1.0, ALU.mult, ALU.mult), s_dv)

        for dst, src in ((ident, identc), (maskT, maskTc), (ones1, ones1c),
                         (mgrp, mgrpc), (sflag, sflagc), (rflag, rflagc),
                         (mn32, mn_in)):
            b.dma("sync", dst[:, :], src[:, :], SY)
        b.dma("sync", grow[0:1, :], gate_in[:, :], SY)
        b.dma("sync", rbrow[0:1, :], rb_in[:, :], SY)
        tconsts = SY.val()
        t_rwbf = b.dma("gpsimd", rw_bf[:, :], rw_in[:, :], GW)
        t_ebbf = b.dma("gpsimd", eb_bf[:, :], eb_in[:, :], GW)
        t_mnbf = b.dma("gpsimd", mn_bf[:, :], mn_in[:, :], GW)

        b.wait("tensor", t_rwbf, tconsts)
        for dt in range(ND):
            tp = b.run("tensor", lambda e, dt=dt: e.transpose(
                PT[:, 0:64], rw_bf[0:64, dt * 128:(dt + 1) * 128],
                ident[0:64, 0:64]),
                s_pe)
            b.wait("vector", tp)
            tv = b.run("vector", lambda e, dt=dt: e.tensor_copy(
                rwT[:, dt * 64:(dt + 1) * 64], PT[:, 0:64]), s_dv)
            b.wait("tensor", tv)
        b.wait("tensor", t_mnbf)
        tp = b.run("tensor", lambda e: e.transpose(PT[:, 64:128], mn_bf[:, :],
                                                   ident[0:64, 0:64]), s_pe)
        b.wait("vector", tp)
        t_mnT = b.run("vector", lambda e: e.tensor_copy(mnT[:, :], PT[:, 64:128]),
                      s_dv)

        b.wait("vector", tconsts)
        t1 = b.run("vector", lambda e: e.tensor_copy(rbrow_bf[:, :],
                                                     rbrow[0:1, :]), s_dv)
        b.wait("tensor", t1)
        t2 = b.run("tensor", lambda e: e.matmul(P[6][:, 0:64], ones1[:, :],
                                                rbrow_bf[:, :], start=True,
                                                stop=True), s_pe)
        b.wait("vector", t2)
        b.run("vector", lambda e: e.tensor_copy(rb_bc[:, :], P[6][:, 0:64]), s_dv)

        b.wait("scalar", tconsts)
        t1 = b.run("scalar", lambda e: e.activation(erow[0:1, 0:16], grow[0:1, :],
                                                    AF.Exp, scale=-1.0), s_ac)
        b.wait("vector", t1)
        b.run("vector", lambda e: e.tensor_scalar(erow[0:1, 0:16],
                                                  erow[0:1, 0:16], 1.0, None,
                                                  ALU.add), s_dv)
        b.run("vector", lambda e: e.reciprocal(grow[0:1, :], erow[0:1, 0:16]),
              s_dv)
        t1 = b.run("vector", lambda e: e.tensor_copy(grow_bf[:, :], grow[0:1, :]),
                   s_dv)
        b.wait("tensor", t1)
        t2 = b.run("tensor", lambda e: e.matmul(P[6][:, 64:80], ones1[:, :],
                                                grow_bf[:, :], start=True,
                                                stop=True), s_pe)
        b.wait("vector", t2)
        b.run("vector", lambda e: e.tensor_copy(g_bc[:, :], P[6][:, 64:80]), s_dv)
        b.run("vector", lambda e: e.tensor_scalar(gm1_bc[:, :], g_bc[:, :], -1.0,
                                                  1.0, ALU.mult, ALU.add), s_dv)

        t1 = b.run("vector", lambda e: e.reduce_sum(colt[0:64, 0:1], mn32[:, :],
                                                    axis=AXX), s_dv)
        b.wait("vector", t1)
        t2 = b.run("vector", lambda e: e.tensor_copy(rows_bf[:, :],
                                                     colt[0:64, 0:1]), s_dv)
        b.wait("tensor", t2)
        t3 = b.run("tensor", lambda e: e.matmul(P[6][0:1, 128:132], rows_bf[:, :],
                                                mgrp[:, :], start=True,
                                                stop=True), s_pe)
        b.wait("vector", t3)
        t4 = b.run("vector", lambda e: e.tensor_scalar(
            erow[0:1, 0:4], P[6][0:1, 128:132], 1.0, -EPS_ATT, ALU.mult,
            ALU.add), s_dv)
        b.wait("scalar", t4)
        t5 = b.run("scalar", lambda e: e.sign(erow[0:1, 4:8], erow[0:1, 0:4]),
                   s_ac)
        b.wait("vector", t5)
        b.run("vector", lambda e: e.tensor_scalar(erow[0:1, 4:8], erow[0:1, 4:8],
                                                  -1.0, 0.0, ALU.mult, ALU.max),
              s_dv)
        b.run("vector", lambda e: e.tensor_scalar(eU[0:1, :], erow[0:1, 4:8],
                                                  -1.0, 1.0, ALU.mult, ALU.add),
              s_dv)
        for z in (qsum4, sqc, sumc, ccsb):
            b.run("vector", lambda e, z=z: e.memset(z[:, :], 0.0), s_dv)

        class _Cut(Exception):
            pass

        def cut(k):
            if PH <= k:
                dump_dbg()
                junk_finish()
                raise _Cut()

        def dump_dbg():
            if not DEBUG:
                return
            b.wait("gpsimd", b.val(s_dv), b.val(s_ac), b.val(s_pe),
                   SY.val(), SC.val())
            b.dma("gpsimd", dbg_st[:, 0:64], lgsb[:, :], GP)
            b.dma("gpsimd", dbg_st[:, 64:72], top8[:, :], GP)
            b.dma("gpsimd", dbg_st[:, 72:88], swtc[:, :], GP)
            b.dma("gpsimd", dbg_st[:, 88:104], esb[:, 0:16], GP)
            b.dma("gpsimd", dbg_st[:, 104:120], rstd0[:, :], GP)
            if PH == 3:
                b.dma("gpsimd", dbg_x2[0:64, :], msT[:, :], GP)
                b.dma("gpsimd", dbg_x2[128:256, :], bigA[0][:, :], GP)
                b.dma("gpsimd", dbg_x2[256:384, :], xln_tm[0][1792:1920, :], GP)
            if PH == 7:
                b.dma("gpsimd", dbg_st[0:1, 0:64], relrow[0:1, :], GP)
                b.dma("gpsimd", dbg_st[0:128, 64:80], g_bc[:, :], GP)
                b.dma("gpsimd", dbg_st[0:128, 80:84], wbc[:, :], GP)
                b.dma("gpsimd", dbg_st[0:128, 84:92], ncol[:, :], GP)
            if PH in (5, 7):
                src_d = out_hbm if PH == 7 else kp_hbm
                for hh_ in range(16):
                    b.dma("gpsimd",
                          dbg_x1[hh_ * 128:(hh_ + 1) * 128, :].rearrange(
                              "p (n e) -> p n e", e=128),
                          v_hbm[hh_][:, :].rearrange("(n p) e -> p n e",
                                                     p=128), GP)
                for hh_ in range(16):
                    b.dma("gpsimd",
                          dbg_x2[hh_ * 128:(hh_ + 1) * 128, :].rearrange(
                              "p (n e) -> p n e", e=128),
                          src_d[hh_][:, :].rearrange("(n p) e -> p n e",
                                                     p=128), GP)
            b.wait("gpsimd", GP.val())

        def junk_finish():
            toks = []
            for tt in range(NT):
                toks.append(b.dma("sync", y_out[tt * 128:(tt + 1) * 128, :],
                                  x0[tt * 128:(tt + 1) * 128, :], FIN))
            b.wait("sync", toks[-1])
            b.wait("scalar", toks[-1])
            b.wait("gpsimd", toks[-1])
            b.wait("vector", toks[-1])
            b.wait("tensor", toks[-1])

        try:
            xtl = [xt0, xt1]
            ld = [None] * NT
            ld[0] = b.dma("sync", xtl[0][:, :], x0[0:128, :], SY)
            ld[1] = b.dma("sync", xtl[1][:, :], x0[128:256, :], SY)
            prev_st = [None, None]
            gate_tok = None
            for tt in range(NT):
                xt = xtl[tt % 2]
                t6 = ln_stats(xt, rstd0, negmr0, tt, tt, ld[tt])
                b.wait("vector", prev_st[tt % 2], t6)
                tcp = b.run("vector", lambda e, tt=tt, xt=xt: e.tensor_scalar(
                    xlnst[tt % 2][:, :], xt[:, :], rstd0[:, tt:tt + 1],
                    negmr0[:, tt:tt + 1], ALU.mult, ALU.add), s_dv)
                b.wait("scalar", tcp)
                st = b.dma("scalar", xln_tm[0][tt * 128:(tt + 1) * 128, :],
                           xlnst[tt % 2][:, :], SC)
                if tt == 0:
                    gate_tok = st
                prev_st[tt % 2] = st
                if tt + 2 < NT:
                    b.wait("sync", tcp)
                    ld[tt + 2] = b.dma("sync", xt[:, :],
                                       x0[(tt + 2) * 128:(tt + 3) * 128, :], SY)
            b.wait("gpsimd", gate_tok)
            wtok = {n: b.dma("gpsimd", wbf[n][:, :], w_in[n][:, :], GW)
                    for n in "qkvo"}
            t_xln0_done = SC.val()
            cut(1)

            xb = [None] * ND
            for dt in range(ND):
                b.wait("scalar", t_xln0_done)
                xb[dt] = b.xbar("scalar", bigA[dt][:, :],
                                xln_tm[0][:, dt * 128:(dt + 1) * 128], SC)
            b.wait("tensor", t_ebbf)
            for tt in range(NT):
                for dt in range(ND):
                    if tt == 0:
                        b.wait("tensor", xb[dt])
                    t_router = b.run("tensor", lambda e, tt=tt, dt=dt: e.matmul(
                        P[tt // 8][:, (tt % 8) * 64:(tt % 8) * 64 + 64],
                        bigA[dt][:, tt * 128:(tt + 1) * 128],
                        rwT[:, dt * 64:(dt + 1) * 64],
                        start=(dt == 0), stop=(dt == ND - 1)), s_pe)
            cut(2)

            state = {"prev_tail_act": None, "prev_tail_dve": None,
                     "prev_xt1_free": None, "prev_st": [None, None],
                     "prev_adds": None}

            def ffn_tail(tt, rstd_a, negmr_a, ci, phase2, xln_dst, x_dst, scol,
                         dbg_dst):
                b.wait("vector", state["prev_tail_act"], t_router,
                       b.val(s_pe), b.val(s_ac))
                b.wait("scalar", state["prev_tail_dve"], b.val(s_dv),
                       b.val(s_pe))
                b.wait("tensor", b.val(s_dv), b.val(s_ac))
                if KSUB == 10:
                    b.run("vector", lambda e: e.tensor_tensor(
                        lgsb[:, :], rb_bc[:, :], rb_bc[:, :], ALU.add), s_dv)
                    return None
                b.run("vector", lambda e, tt=tt: e.tensor_tensor(
                    lgsb[:, :], P[tt // 8][:, (tt % 8) * 64:(tt % 8) * 64 + 64],
                    rb_bc[:, :], ALU.add), s_dv)
                if KSUB == 11:
                    return None
                b.run("vector", lambda e: e.max(out=top8[:, :], in_=lgsb[:, :]), s_dv)
                t1 = b.run("vector", lambda e: e.tensor_scalar(
                    colt[:, 3:4], top8[:, 0:1], -1.0, None, ALU.mult), s_dv)
                if KSUB == 12:
                    return None
                b.wait("scalar", t1)
                t2 = b.run("scalar", lambda e: e.activation(
                    esb[:, :], lgsb[:, :], AF.Exp, bias=colt[:, 3:4]), s_ac)
                t3 = b.run("scalar", lambda e: e.activation(
                    e8sb[:, :], top8[:, :], AF.Exp, bias=colt[:, 3:4]), s_ac)
                b.wait("vector", t2, t3)
                if KSUB == 13:
                    return None
                b.run("vector", lambda e: e.reduce_sum(colt[:, 4:5], esb[:, :],
                                                       axis=AXX), s_dv)
                b.run("vector", lambda e: e.reduce_sum(colt[:, 5:6], e8sb[:, :],
                                                       axis=AXX), s_dv)
                b.run("vector", lambda e: e.reciprocal(colt[:, 6:7], colt[:, 4:5]),
                      s_dv)
                b.run("vector", lambda e: e.tensor_scalar(
                    colt[:, 6:7], colt[:, 6:7], 0.5, None, ALU.mult), s_dv)
                b.run("vector", lambda e, tt=tt: e.tensor_scalar(
                    swtc[:, tt:tt + 1], colt[:, 5:6], colt[:, 6:7], None, ALU.mult),
                    s_dv)
                if KSUB == 14:
                    return None
                b.run("vector", lambda e: e.tensor_scalar(
                    mskb[:, :], lgsb[:, :], top8[:, 7:8], None, ALU.is_ge), s_dv)
                t4 = b.run("vector", lambda e: e.scalar_tensor_tensor(
                    msbf[:, :], esb[:, :], colt[:, 6:7], mskb[:, :], ALU.mult,
                    ALU.mult), s_dv)
                state["prev_tail_act"] = t3
                state["prev_tail_dve"] = t4
                if KSUB < 2:
                    return None
                b.wait("tensor", t4)
                t5 = b.run("tensor", lambda e: e.transpose(
                    PT[0:64, 128:256], msbf[:, :], ident[:, :]), s_pe)
                b.wait("vector", t5)
                t6 = b.run("vector", lambda e, tt=tt: e.tensor_copy(
                    msT[:, tt * 128:(tt + 1) * 128], PT[0:64, 128:256]), s_dv)
                b.wait("tensor", t6, state.get("prev_adds"))
                if KSUB < 3:
                    return None
                mm = []
                for ec in range(4):
                    mm.append(b.run("tensor", lambda e, tt=tt, ec=ec: e.matmul(
                        P[2 + ec][:, :], msT[:, tt * 128:(tt + 1) * 128],
                        eb_bf[:, ec * 512:(ec + 1) * 512], start=True, stop=True),
                        s_pe))
                if KSUB < 4:
                    return None
                b.run("vector", lambda e, tt=tt: e.tensor_scalar(
                    colt[:, 7:8], swtc[:, tt:tt + 1], rstd_a[:, ci:ci + 1], None,
                    ALU.mult), s_dv)
                t7 = b.run("vector", lambda e: e.tensor_scalar(
                    colt[:, 7:8], colt[:, 7:8], 1.0, None, ALU.add), s_dv)
                t8 = b.run("vector", lambda e, tt=tt: e.tensor_scalar(
                    colt[:, 2:3], swtc[:, tt:tt + 1], negmr_a[:, ci:ci + 1], None,
                    ALU.mult), s_dv)
                b.wait("vector", t7, t8, state["prev_xt1_free"],
                       state.get("prev_xt1_b"))
                t9 = b.run("vector", lambda e: e.tensor_scalar(
                    xt1[:, :], xt0[:, :], colt[:, 7:8], colt[:, 2:3],
                    ALU.mult, ALU.add), s_dv)
                b.wait("vector", *mm)
                adds = []
                for ec in range(4):
                    adds.append(b.run("vector", lambda e, ec=ec: e.tensor_tensor(
                        xt1[:, ec * 512:(ec + 1) * 512],
                        xt1[:, ec * 512:(ec + 1) * 512], P[2 + ec][:, :], ALU.add),
                        s_dv))
                state["prev_tail_act"] = t9
                state["prev_tail_dve"] = adds[-1]
                state["prev_adds"] = adds[-1]
                if KSUB < 5:
                    state["prev_tail_dve"] = adds[-1]
                    return None
                if phase2:
                    b.wait("scalar", *adds)
                    st = b.dma("scalar", y_out[tt * 128:(tt + 1) * 128, :],
                               xt1[:, :], FIN)
                    state["prev_xt1_free"] = st
                    return t9
                tln = ln_stats(xt1, rstd1, negmr1, tt % 2, scol, adds[-1])
                b.wait("vector", tln, state["prev_st"][tt % 2])
                tcp = b.run("vector", lambda e, tt=tt: e.tensor_scalar(
                    xlnst[tt % 2][:, :], xt1[:, :],
                    rstd1[:, (tt % 2):(tt % 2) + 1],
                    negmr1[:, (tt % 2):(tt % 2) + 1], ALU.mult, ALU.add), s_dv)
                b.wait("scalar", tcp)
                st1 = b.dma("scalar", xln_dst[tt * 128:(tt + 1) * 128, :],
                            xlnst[tt % 2][:, :], SC)
                st2 = b.dma("scalar", x_dst[tt * 128:(tt + 1) * 128, :], xt1[:, :],
                            SC)
                if dbg_dst is not None:
                    st2 = b.dma("scalar", dbg_dst[tt * 128:(tt + 1) * 128, :],
                                xt1[:, :], FIN)
                    b.wait("gpsimd", tcp)
                    st2b = b.dma("gpsimd", dbg_x2[tt * 128:(tt + 1) * 128, :],
                                 xt1[:, :], GP)
                    state["prev_xt1_b"] = st2b
                state["prev_st"][tt % 2] = st1
                state["prev_xt1_free"] = st2
                state["prev_tail_act"] = tcp
                return tcp

            b.wait("sync", b.val(s_dv), b.val(s_ac))
            prev_ld_reader = None
            for tt in range(NT):
                b.wait("sync", prev_ld_reader)
                ldx = b.dma("sync", xt0[:, :], x0[tt * 128:(tt + 1) * 128, :], SY)
                b.wait("vector", ldx)
                b.wait("scalar", ldx)
                prev_ld_reader = ffn_tail(
                    tt, rstd0, negmr0, tt, False, xln_tm[1], x1_hbm, 16 + tt,
                    dbg_x1 if DEBUG else None)

            cut(3)
            barrier()

            xb1 = [None] * ND
            for dt in range(ND):
                xb1[dt] = b.xbar("scalar", bigA[dt][:, :],
                                 xln_tm[1][:, dt * 128:(dt + 1) * 128], SC)

            def stream_wT(name, half):
                toks = []
                dep = b.val(s_pe)
                for dt in range(ND):
                    b.wait("scalar", wtok[name], dep)
                    toks.append(b.xbar("scalar", wT[dt][:, :],
                                       wbf[name][half * 1024:(half + 1) * 1024,
                                                 dt * 128:(dt + 1) * 128], SC))
                return toks

            prev_stt = [None, None]
            for half in range(2):
                wts = stream_wT("q", half)
                prev_evac = {}
                for ei in range(8):
                    et = half * 8 + ei
                    for pair in range(2):
                        bk = (P[0], P[1]) if pair == 0 else (P[2], P[3])
                        b.wait("tensor", prev_evac.get(pair * 2),
                               prev_evac.get(pair * 2 + 1))
                        for dt in range(ND):
                            if ei == 0 and pair == 0:
                                b.wait("tensor", wts[dt], xb1[dt])
                            for tci in range(2):
                                tc = pair * 2 + tci
                                tmm_t = b.run("tensor", lambda e, et=et, dt=dt,
                                              tc=tc, bb=bk[tci]: e.matmul(
                                          bb[:, :],
                                          wT[dt][:, (et % 8) * 128:
                                                 (et % 8) * 128 + 128],
                                          bigA[dt][:, tc * 512:(tc + 1) * 512],
                                          start=(dt == 0), stop=(dt == ND - 1)),
                                      s_pe)
                        for tci in range(2):
                            tc = pair * 2 + tci
                            bb = bk[tci]
                            p = tc % 2
                            b.wait("scalar", tmm_t, prev_stt[p])
                            ta = b.run("scalar", lambda e, bb=bb, p=p, et=et,
                                       tc=tc:
                                  e.activation(qsb[p][:, :], bb[:, :], AF.Copy,
                                               accum_out=qsum4[:, et * 4 + tc:
                                                               et * 4 + tc + 1]),
                                  s_ac)
                            b.wait("vector", ta)
                            tb = b.run("vector", lambda e, p=p: e.tensor_scalar(
                                etm[p][:, 0:512], qsb[p][:, :], 0.0, None, ALU.min),
                                s_dv)
                            b.wait("scalar", tb)
                            tc2 = b.run("scalar", lambda e, p=p: e.activation(
                                etm[p][:, 512:1024], etm[p][:, 0:512], AF.Exp),
                                s_ac)
                            b.wait("vector", tc2)
                            td = b.run("vector", lambda e, p=p, et=et, tc=tc:
                                       e.scalar_tensor_tensor(
                                           bigB[et][:, tc * 512:(tc + 1) * 512],
                                           qsb[p][:, :], 0.0, etm[p][:, 512:1024],
                                           ALU.max, ALU.add), s_dv)
                            prev_evac[tc] = td
                            prev_stt[p] = td

            t1 = b.run("vector", lambda e: e.reduce_sum(
                qsum_f[:, :], qsum4[:, :].rearrange("p (h f) -> p h f", f=4),
                axis=AXX), s_dv)
            b.wait("vector", t1)
            t_qs = b.run("vector", lambda e: e.tensor_copy(qsum_bf[:, :],
                                                           qsum_f[:, :]), s_dv)
            b.wait("tensor", t_qs, t_mnT)
            for h in range(H):
                t_rel = b.run("tensor", lambda e, h=h: e.matmul(
                    P[6][0:1, 256 + h * 4:256 + h * 4 + 4], qsum_bf[:, h:h + 1],
                    mnT[:, h * 4:h * 4 + 4], start=True, stop=True), s_pe)
            b.wait("vector", t_rel)
            t_relv = b.run("vector", lambda e: e.tensor_scalar(
                relrow[0:1, :], P[6][0:1, 256:320], SCALE / 4096.0, None, ALU.mult),
                s_dv)

            cut(4)
            def proj_b(name, dest_hbm, do_epo):
                prev_sp_slot = [None, None]
                for half in range(2):
                    wts = stream_wT(name, half)
                    prev_ev = [None, None]
                    for tt in range(NT):
                        ps_ = tt % 2
                        bk = (P[0], P[1]) if ps_ == 0 else (P[2], P[3])
                        b.wait("tensor", prev_ev[ps_])
                        for dt in range(ND):
                            if tt == 0:
                                b.wait("tensor", wts[dt], xb1[dt])
                            for eci in range(2):
                                tmm_t = b.run("tensor", lambda e, tt=tt, dt=dt,
                                              eci=eci, bb=bk[eci]: e.matmul(
                                          bb[:, :],
                                          bigA[dt][:, tt * 128:(tt + 1) * 128],
                                          wT[dt][:, eci * 512:(eci + 1) * 512],
                                          start=(dt == 0), stop=(dt == ND - 1)),
                                      s_pe)
                        last = None
                        for eci in range(2):
                            bb = bk[eci]
                            p = tt % 2
                            if do_epo:
                                b.wait("vector", tmm_t, prev_sp_slot[p])
                                tb = b.run("vector", lambda e, bb=bb, p=p:
                                           e.tensor_scalar(etm[p][:, 0:512],
                                                           bb[:, :], 0.0, None,
                                                           ALU.min), s_dv)
                                b.wait("scalar", tb)
                                tc2 = b.run("scalar", lambda e, p=p: e.activation(
                                    etm[p][:, 512:1024], etm[p][:, 0:512], AF.Exp),
                                    s_ac)
                                b.wait("vector", tc2)
                                last = b.run(
                                    "vector", lambda e, bb=bb, p=p, eci=eci:
                                    e.scalar_tensor_tensor(
                                        kvst[p][:, eci * 512:(eci + 1) * 512],
                                        bb[:, :], 0.0, etm[p][:, 512:1024],
                                        ALU.max, ALU.add), s_dv)
                            else:
                                b.wait("vector", tmm_t, prev_sp_slot[p])
                                last = b.run("vector", lambda e, bb=bb, p=p,
                                             eci=eci: e.tensor_copy(
                                                 kvst[p][:, eci * 512:
                                                         (eci + 1) * 512],
                                                 bb[:, :]), s_dv)
                            prev_ev[ps_] = last
                        b.wait("sync", last)
                        prev_sp_slot[tt % 2] = b.dma(
                            "sync",
                            dest_hbm[half * 8:(half + 1) * 8,
                                     tt * 128:(tt + 1) * 128, :].rearrange(
                                         "h p e -> p h e"),
                            kvst[tt % 2][:, :].rearrange("p (h e) -> p h e", e=128),
                            SY)
                return prev_sp_slot

            proj_b("k", kp_hbm, True)
            proj_b("v", v_hbm, False)

            cut(5)
            barrier()

            for i in range(2):
                b.run("vector", lambda e, i=i: e.memset(
                    vaug[i][:, :].rearrange("p (n c) -> p n c", c=132)[:, :, 128:129],
                    1.0), s_dv)
            t_vone = b.val(s_dv)

            attn_state = {"prev_Lcopy": None}

            def emit_A(h, dep_tensor, dep_vec):
                sl = h % 2
                b.wait("sync", dep_tensor)
                tk = b.dma("sync",
                           kpsd[sl][:, :].rearrange("p (n e) -> p n e", e=128),
                           kp_hbm[h][:, :].rearrange("(n p) e -> p n e", p=128),
                           SY)
                tv = b.dma("sync",
                           vaug[sl][:, :].rearrange("p (n c) -> p n c",
                                                    c=132)[:, :, 0:128],
                           v_hbm[h][:, :].rearrange("(n p) e -> p n e", p=128),
                           SY)
                b.wait("tensor", tk, tv, t_vone)
                b.wait("scalar", dep_vec)
                tcopy = attn_state["prev_Lcopy"]
                for j in range(NCH):
                    if tcopy is not None:
                        b.wait("tensor", tcopy)
                    tj = b.run("tensor", lambda e, sl=sl, j=j: e.matmul(
                        P[5][:, 0:129], kpsd[sl][:, j * 128:(j + 1) * 128],
                        vaug[sl][:, j * 132:j * 132 + 129],
                        start=(j == 0), stop=(j == NCH - 1)), s_pe)
                    b.wait("scalar", tj)
                    tcopy = b.run("scalar", lambda e, sl=sl, j=j: e.activation(
                        Lb[sl][:, j * 129:(j + 1) * 129], P[5][:, 0:129],
                        AF.Copy), s_ac)
                attn_state["prev_Lcopy"] = tcopy
                b.wait("vector", GP.val(), tcopy)
                t1 = b.run("vector", lambda e, sl=sl: e.tensor_scalar(
                    ccsb[:, 0:129], Lb[sl][:, 15 * 129:16 * 129], sflag[:, 0:1],
                    None, ALU.mult), s_dv)
                b.wait("vector", t_relv)
                t2 = b.run("vector", lambda e, h=h: e.tensor_copy(
                    ccsb[0:1, 129:133], relrow[0:1, h * 4:h * 4 + 4]), s_dv)
                b.wait("gpsimd", t1, t2)
                b.dma("gpsimd", cc_in[h].ap(), ccsb[:, :], GP)
                sem, prior, newv = CC.next_slot()
                b.em("gpsimd", lambda e, h=h, s=sem, p=prior: (
                    e.wait_ge(s, p) if p > 0 else None,
                    e.collective_compute(
                        "AllReduce", ALU.add,
                        replica_groups=[[0, 1], [2, 3], [4, 5], [6, 7]],
                        ins=[cc_in[h].ap().opt()], outs=[cc_out[h].ap().opt()]
                    ).then_inc(s, 1)))
                return (sem, newv)

            def emit_B_hdr(h, st, tcc, dep_dve, dep_sp):
                sl = h % 2
                b.wait("vector", b.val(s_ac), b.val(s_pe), SC.val(), dep_sp)
                b.wait("scalar", b.val(s_dv))
                b.wait("gpsimd", b.val(s_pe))
                tmh = b.dma("gpsimd", memh[sl][:, :].rearrange(
                    "p (m e) -> p m e", e=128),
                    mems[:, h].rearrange("m p e -> p m e"), GW)
                b.wait("sync", b.val(s_pe))
                tds = b.xbar("sync", kpds[sl][:, :], kp_hbm[h][:, :], SY)
                b.wait("gpsimd", tcc, dep_dve)
                ccrv, wg, wg_bf, wbc = ccrvS[st], wgS[st], wg_bfS[st], wbcS[st]
                wcol = 320 + st * 8
                trv = b.dma("gpsimd", ccrv[:, :], cc_out[h].ap(), GP)
                b.wait("vector", trv)
                tse = b.run("vector", lambda e, st=st: e.tensor_scalar(
                    steffS[st][:, :], ccrvS[st][:, 0:129], rflag[:, 0:1], None,
                    ALU.mult), s_dv)
                b.run("vector", lambda e: e.tensor_tensor(
                    wg[0:1, 0:4], ccrv[0:1, 129:133], eU[0:1, :], ALU.mult), s_dv)
                b.run("vector", lambda e: e.scalar_tensor_tensor(
                    wg[0:1, 0:4], erow[0:1, 4:8], -1e30, wg[0:1, 0:4], ALU.mult,
                    ALU.add), s_dv)
                b.run("vector", lambda e: e.reduce_max(wg[0:1, 4:5], wg[0:1, 0:4],
                                                       axis=AXX), s_dv)
                t1 = b.run("vector", lambda e: e.tensor_scalar(
                    wg[0:1, 5:6], wg[0:1, 4:5], -1.0, None, ALU.mult), s_dv)
                b.wait("scalar", t1)
                t2 = b.run("scalar", lambda e: e.activation(
                    wg[0:1, 0:4], wg[0:1, 0:4], AF.Exp, bias=wg[0:1, 5:6]), s_ac)
                b.wait("vector", t2)
                b.run("vector", lambda e: e.reduce_sum(wg[0:1, 6:7], wg[0:1, 0:4],
                                                       axis=AXX), s_dv)
                b.run("vector", lambda e: e.reciprocal(wg[0:1, 7:8], wg[0:1, 6:7]),
                      s_dv)
                b.run("vector", lambda e: e.tensor_scalar(
                    wg[0:1, 0:4], wg[0:1, 0:4], wg[0:1, 7:8], None, ALU.mult), s_dv)
                b.run("vector", lambda e: e.tensor_tensor(
                    wg[0:1, 0:4], wg[0:1, 0:4], eU[0:1, :], ALU.mult), s_dv)
                b.run("vector", lambda e, h=h: e.tensor_scalar(
                    wg[0:1, 0:4], wg[0:1, 0:4], g_bc[0:1, h:h + 1], None, ALU.mult),
                    s_dv)
                t3 = b.run("vector", lambda e: e.tensor_copy(wg_bf[:, :],
                                                             wg[0:1, 0:4]), s_dv)
                b.wait("tensor", t3)
                t4 = b.run("tensor", lambda e, wcol=wcol: e.matmul(
                    P[6][:, wcol:wcol + 4], ones1[:, :], wg_bf[:, :], start=True,
                    stop=True), s_pe)
                b.wait("vector", t4)
                t_wbc = b.run("vector", lambda e, wcol=wcol: e.tensor_copy(
                    wbc[:, :], P[6][:, wcol:wcol + 4]), s_dv)
                return dict(h=h, st=st, sl=sl, tse=tse, tds=tds, tmh=tmh,
                            t_wbc=t_wbc, prev_tact=None, last_dve=None,
                            last_tintra=None)

            def emit_B_j(C, j):
                h, st, sl = C["h"], C["st"], C["sl"]
                stats, steff, scb = statsS[st], steffS[st], scbS[st]
                umm, outst = ummS[st], outstS[st]
                dcol, ncol, wbc = dcolS[st], ncolS[st], wbcS[st]
                PA = P[4] if st == 0 else P[3]
                Pn = P[st]
                cj = (j % 2) * 256
                sc = st * 256 + (j % 2) * 128
                nsl = 384 + st * 32 + (j % 2) * 8
                if j == 0:
                    b.wait("vector", C["tse"])
                    ts = b.run("vector", lambda e, stats=stats, steff=steff:
                               e.tensor_copy(stats[:, 0:129], steff[:, :]), s_dv)
                else:
                    ts = b.run("vector", lambda e, sl=sl, j=j, stats=stats,
                               steff=steff: e.tensor_tensor(
                        stats[:, j * 129:(j + 1) * 129],
                        Lb[sl][:, (j - 1) * 129:j * 129], steff[:, :], ALU.add),
                        s_dv)
                b.wait("tensor", ts)
                if j == 0:
                    b.wait("tensor", C["tds"], C["tmh"])
                b.run("tensor", lambda e, h=h, j=j, cj=cj, Pn=Pn, stats=stats:
                      e.matmul(
                    Pn[:, cj:cj + 129], bigB[h][:, j * 128:(j + 1) * 128],
                    stats[:, j * 129:(j + 1) * 129], start=True, stop=False),
                    s_pe)
                b.wait("tensor", C["prev_tact"])
                b.run("tensor", lambda e, h=h, j=j, sl=sl, PA=PA: e.matmul(
                    PA[:, 0:512], bigB[h][:, j * 128:(j + 1) * 128],
                    memh[sl][:, 0:512], start=True, stop=True), s_pe)
                b.run("tensor", lambda e, h=h, j=j, nsl=nsl: e.matmul(
                    P[6][:, nsl:nsl + 4], bigB[h][:, j * 128:(j + 1) * 128],
                    mnT[:, h * 4:h * 4 + 4], start=True, stop=True), s_pe)
                tsc = b.run("tensor", lambda e, h=h, j=j, sl=sl, sc=sc: e.matmul(
                    P[2][:, sc:sc + 128],
                    kpds[sl][:, j * 128:(j + 1) * 128],
                    bigB[h][:, j * 128:(j + 1) * 128], start=True, stop=True),
                    s_pe)
                b.wait("vector", tsc)
                tm1 = b.run("vector", lambda e, sc=sc, scb=scb: e.tensor_tensor(
                    scb[:, :], P[2][:, sc:sc + 128], maskT[:, :], ALU.mult),
                    s_dv)
                b.wait("tensor", tm1)
                tintra = b.run("tensor", lambda e, j=j, sl=sl, cj=cj, Pn=Pn,
                               scb=scb: e.matmul(
                    Pn[:, cj:cj + 129], scb[:, :],
                    vaug[sl][:, j * 132:j * 132 + 129], start=False, stop=True),
                    s_pe)
                C["last_tintra"] = tintra
                b.wait("vector", tintra, C["t_wbc"])
                b.run("vector", lambda e, cj=cj, Pn=Pn, dcol=dcol: e.tensor_scalar(
                    dcol[:, 0:1], Pn[:, cj + 128:cj + 129], EPS_ATT, None, ALU.max),
                    s_dv)
                b.run("vector", lambda e, dcol=dcol: e.reciprocal(
                    dcol[:, 1:2], dcol[:, 0:1]), s_dv)
                b.run("vector", lambda e, h=h, dcol=dcol: e.tensor_scalar(
                    dcol[:, 2:3], dcol[:, 1:2], gm1_bc[:, h:h + 1], None,
                    ALU.mult), s_dv)
                b.run("vector", lambda e, nsl=nsl, ncol=ncol: e.tensor_scalar(
                    ncol[:, 0:4], P[6][:, nsl:nsl + 4], EPS_ATT, None, ALU.max),
                    s_dv)
                b.run("vector", lambda e, ncol=ncol: e.reciprocal(
                    ncol[:, 4:8], ncol[:, 0:4]), s_dv)
                b.run("vector", lambda e, ncol=ncol, wbc=wbc: e.tensor_tensor(
                    ncol[:, 0:4], ncol[:, 4:8], wbc[:, :], ALU.mult), s_dv)
                tacts = []
                for m in range(4):
                    src = umm[:, 128:256] if m % 2 == 0 else umm[:, 0:128]
                    dst = umm[:, 0:128] if m % 2 == 0 else umm[:, 128:256]
                    if m == 0:
                        tacts.append(b.run("vector", lambda e, m=m, dst=dst,
                            PA=PA, ncol=ncol: e.tensor_scalar(
                                dst, PA[:, m * 128:(m + 1) * 128],
                                ncol[:, m:m + 1], None, ALU.mult), s_dv))
                    else:
                        tacts.append(b.run("vector", lambda e, m=m, dst=dst,
                            src=src, PA=PA, ncol=ncol: e.scalar_tensor_tensor(
                                dst, PA[:, m * 128:(m + 1) * 128],
                                ncol[:, m:m + 1], src,
                                ALU.mult, ALU.add), s_dv))
                C["prev_tact"] = tacts[3]
                C["last_dve"] = b.run(
                    "vector", lambda e, j=j, cj=cj, Pn=Pn, outst=outst, dcol=dcol,
                    umm=umm: e.scalar_tensor_tensor(
                        outst[:, j * 128:(j + 1) * 128],
                        Pn[:, cj:cj + 128], dcol[:, 2:3], umm[:, 128:256],
                        ALU.mult, ALU.add), s_dv)

            def emit_B_tail(C):
                h, st = C["h"], C["st"]
                b.wait("sync", C["last_dve"])
                tsp = b.dma("sync", out_hbm[h][:, :].rearrange(
                    "(n p) e -> p n e", p=128),
                    outstS[st][:, :].rearrange("p (n e) -> p n e", e=128), SY)
                b.wait("sync", tsp)
                txb = b.xbar("sync", bigB[h][:, :], out_hbm[h][:, :], SY)
                C["tail_sp"] = tsp
                return txb

            tccs = [None] * H
            xb_out = [None] * H
            tccs[0] = emit_A(0, None, None)
            tccs[1] = emit_A(1, None, None)
            prev_dve = [None, None]
            prev_sp = [None, None]
            for hp in range(0, H, 2):
                CA = emit_B_hdr(hp, 0, tccs[hp], prev_dve[0], prev_sp[0])
                CB = emit_B_hdr(hp + 1, 1, tccs[hp + 1], prev_dve[1], prev_sp[1])
                for j in range(NCH):
                    emit_B_j(CA, j)
                    emit_B_j(CB, j)
                xb_out[hp] = emit_B_tail(CA)
                xb_out[hp + 1] = emit_B_tail(CB)
                prev_dve = [CA["last_dve"], CB["last_dve"]]
                prev_sp = [CA["tail_sp"], CB["tail_sp"]]
                if hp + 2 < H:
                    tccs[hp + 2] = emit_A(hp + 2, CA["last_tintra"],
                                          CA["last_dve"])
                if hp + 3 < H:
                    tccs[hp + 3] = emit_A(hp + 3, CB["last_tintra"],
                                          CB["last_dve"])

            cut(6)
            barrier()

            xbo = [None] * ND
            for dt in range(ND):
                eng, s = (("sync", SY) if dt % 2 == 0 else ("scalar", SC))
                b.wait(eng, wtok["o"])
                xbo[dt] = b.xbar(eng, bigA[dt][:, :],
                                 wbf["o"][:, dt * 128:(dt + 1) * 128], s)
            prev_ev = [None, None, None, None]
            prev_rd = None
            prev_sp2 = [None, None]
            for tt in range(NT):
                b.wait("scalar", prev_rd)
                ldx1 = b.dma("scalar", xt0[:, :],
                             x1_hbm[tt * 128:(tt + 1) * 128, :], SC)
                tadds = []
                for ec in range(4):
                    b.wait("tensor", prev_ev[ec])
                    for hh in range(H):
                        if tt == 0 and ec == 0:
                            b.wait("tensor", xbo[hh], xb_out[hh])
                        tg = b.run("tensor", lambda e, tt=tt, hh=hh, ec=ec:
                                   e.matmul(
                            P[ec][:, :], bigB[hh][:, tt * 128:(tt + 1) * 128],
                            bigA[hh][:, ec * 512:(ec + 1) * 512],
                            start=(hh == 0), stop=(hh == H - 1)), s_pe)
                    b.wait("vector", tg, ldx1 if ec == 0 else None,
                           state["prev_xt1_free"] if ec == 0 else None)
                    ta = b.run("vector", lambda e, ec=ec: e.tensor_tensor(
                        xt1[:, ec * 512:(ec + 1) * 512],
                        xt0[:, ec * 512:(ec + 1) * 512], P[ec][:, :], ALU.add),
                        s_dv)
                    tadds.append(ta)
                    prev_ev[ec] = ta
                prev_rd = tadds[-1]
                tln = ln_stats(xt1, rstd2, negmr2, tt, 32 + tt, tadds[-1])
                b.wait("vector", tln, prev_sp2[tt % 2])
                tcp = b.run("vector", lambda e, tt=tt: e.tensor_scalar(
                    xlnst[tt % 2][:, :], xt1[:, :], rstd2[:, tt:tt + 1],
                    negmr2[:, tt:tt + 1], ALU.mult, ALU.add), s_dv)
                b.wait("scalar", tcp)
                st1 = b.dma("scalar", xln_tm[2][tt * 128:(tt + 1) * 128, :],
                            xlnst[tt % 2][:, :], SC)
                st2 = b.dma("scalar", x2_hbm[tt * 128:(tt + 1) * 128, :], xt1[:, :],
                            SC)
                if DEBUG:
                    b.dma("scalar", dbg_x2[tt * 128:(tt + 1) * 128, :], xt1[:, :],
                          FIN)
                prev_sp2[tt % 2] = st1
                state["prev_xt1_free"] = st2

            cut(7)
            barrier()

            xb2 = [None] * ND
            for dt in range(ND):
                xb2[dt] = b.xbar("scalar", bigA[dt][:, :],
                                 xln_tm[2][:, dt * 128:(dt + 1) * 128], SC)
            for tt in range(NT):
                for dt in range(ND):
                    if tt == 0:
                        b.wait("tensor", xb2[dt])
                    t_router = b.run("tensor", lambda e, tt=tt, dt=dt: e.matmul(
                        P[tt // 8][:, (tt % 8) * 64:(tt % 8) * 64 + 64],
                        bigA[dt][:, tt * 128:(tt + 1) * 128],
                        rwT[:, dt * 64:(dt + 1) * 64],
                        start=(dt == 0), stop=(dt == ND - 1)), s_pe)

            state["prev_tail_act"] = None
            state["prev_tail_dve"] = None
            prev_ld_reader = None
            for tt in range(NT):
                b.wait("scalar", prev_ld_reader)
                ldx = b.dma("scalar", xt0[:, :],
                            x2_hbm[tt * 128:(tt + 1) * 128, :], SC)
                b.wait("vector", ldx)
                b.wait("scalar", ldx)
                prev_ld_reader = ffn_tail(tt, rstd2, negmr2, tt, True, None, None,
                                          0, None)

            dump_dbg()
            dump_dbg()
            fin_tok = FIN.val()
            b.wait("scalar", fin_tok)
            b.wait("sync", fin_tok)
            b.wait("gpsimd", fin_tok)

        except _Cut:
            pass

        with nc.Block() as block:

            @block.sync
            def _(eng):
                for fn in b.ops["sync"]:
                    fn(eng)

            @block.scalar
            def _(eng):
                for fn in b.ops["scalar"]:
                    fn(eng)

            @block.vector
            def _(eng):
                for fn in b.ops["vector"]:
                    fn(eng)

            @block.tensor
            def _(eng):
                for fn in b.ops["tensor"]:
                    fn(eng)

            @block.gpsimd
            def _(eng):
                for fn in b.ops["gpsimd"]:
                    fn(eng)

    return nc


_NC_CACHE = {}


def kernel(**inputs):
    x = np.ascontiguousarray(np.asarray(inputs["L0_attn_hidden_states"],
                                        dtype=np.float32))
    w_q = np.ascontiguousarray(np.asarray(inputs["L0_attn_w_q"], np.float32))
    w_k = np.ascontiguousarray(np.asarray(inputs["L0_attn_w_k"], np.float32))
    w_v = np.ascontiguousarray(np.asarray(inputs["L0_attn_w_v"], np.float32))
    w_o = np.ascontiguousarray(np.asarray(inputs["L0_attn_w_o"], np.float32))
    memories = np.ascontiguousarray(np.asarray(inputs["L0_attn_memories"],
                                               np.float32))
    mn = np.ascontiguousarray(
        np.asarray(inputs["L0_attn_memory_norms"], np.float32)
        .transpose(1, 0, 2).reshape(H * 4, 128))
    gate = np.asarray(inputs["L0_attn_gate"], np.float32).reshape(1, H)
    r_w = np.ascontiguousarray(np.asarray(inputs["L0_ffn_router_weight"],
                                          np.float32))
    r_b = np.asarray(inputs["L0_ffn_router_bias"], np.float32).reshape(1, 64)
    eb = np.ascontiguousarray(np.asarray(inputs["L0_ffn_expert_bias"],
                                         np.float32))

    identx = np.eye(128, dtype=ml_dtypes.bfloat16)
    maskTx = np.triu(np.ones((128, 128), dtype=np.float32)).astype(
        ml_dtypes.bfloat16)
    mgrpx = np.zeros((64, 4), dtype=ml_dtypes.bfloat16)
    for p in range(64):
        mgrpx[p, p % 4] = 1
    ones1x = np.ones((1, 128), dtype=ml_dtypes.bfloat16)

    if "nc" not in _NC_CACHE:
        _NC_CACHE["nc"] = build_program()
    nc = _NC_CACHE["nc"]

    in_maps = []
    for c in range(8):
        bi, hf = c // 2, c % 2
        sf = np.full((128, 1), 1.0 if hf == 0 else 0.0, np.float32)
        in_maps.append(dict(
            x0=np.ascontiguousarray(x[bi, hf * T:(hf + 1) * T, :]),
            w_q=w_q, w_k=w_k, w_v=w_v, w_o=w_o,
            mems=memories, mn=mn, gate=gate, r_w=r_w, r_b=r_b, eb=eb,
            identc=identx, maskTc=maskTx, mgrpc=mgrpx, ones1c=ones1x,
            sflagc=sf, rflagc=1.0 - sf,
        ))
    res = run_bass_kernel_spmd(nc, in_maps, core_ids=list(range(8)))
    out = np.empty((4, 4096, D), np.float32)
    for c in range(8):
        bi, hf = c // 2, c % 2
        out[bi, hf * T:(hf + 1) * T, :] = res.results[c]["y"]
    if DEBUG:
        kernel.dbg = res.results
    return out


if __name__ == "__main__":
    nc = build_program()
    print("build ok; instructions:", nc.n_instructions()
          if hasattr(nc, "n_instructions") else "?")


# version 06
# speedup vs baseline: 1.0358x; 1.0358x over previous
import numpy as np
import ml_dtypes
import concourse.bass as bass
import concourse.mybir as mybir
from concourse.bass_utils import run_bass_kernel_spmd
from contextlib import ExitStack

F32 = mybir.dt.float32
BF16 = mybir.dt.bfloat16
AF = mybir.ActivationFunctionType
ALU = mybir.AluOpType
AXX = mybir.AxisListType.X

D = 2048
T = 2048
H = 16
NT = 16
ND = 16
NCH = 16
SCALE = 128.0 ** -0.5
EPS_ATT = 1e-6
EPS_LN = 1e-5
DEBUG = False
import os
PH = int(os.environ.get("KPH", "9"))
KSUB = int(os.environ.get("KSUB", "99"))
KATT = int(os.environ.get("KATT", "16"))
KSB = int(os.environ.get("KSB", "99"))


def carve(nc, region, offset_bytes, name, shape, dtype):
    per_part = int(np.prod(shape[1:])) * mybir.dt.size(dtype)
    base = nc.lookup_mloc(region).addr
    assert offset_bytes % 32 == 0
    hnd = nc.alloc_sbuf_tensor_at(name, list(shape), dtype,
                                  offset=base + offset_bytes)
    return hnd, offset_bytes + ((per_part + 31) // 32) * 32


class Stream:

    def __init__(self, sems, step=16):
        self.sems = sems
        self.cnt = [0] * len(sems)
        self.i = 0
        self.step = step

    def next_slot(self):
        k = self.i % len(self.sems)
        self.i += 1
        prior = self.cnt[k]
        self.cnt[k] += self.step
        return self.sems[k], prior, self.cnt[k]

    def val(self):
        return tuple((s, c) for s, c in zip(self.sems, self.cnt) if c > 0)


def _flat(toks):
    out = []
    for t in toks:
        if t is None:
            continue
        if isinstance(t, tuple) and len(t) == 2 and not isinstance(t[0], tuple):
            out.append(t)
        else:
            out.extend(x for x in t if x is not None)
    return tuple(t for t in out if t[1] > 0)


class B:
    ENGS = ("sync", "scalar", "vector", "tensor", "gpsimd")

    def __init__(self):
        self.ops = {k: [] for k in self.ENGS}
        self.cnt = {}

    def _inc(self, sem, n):
        self.cnt[sem.name] = self.cnt.get(sem.name, 0) + n
        return (sem, self.cnt[sem.name])

    def val(self, sem):
        return (sem, self.cnt.get(sem.name, 0))

    def em(self, engine, fn):
        self.ops[engine].append(fn)

    def dma(self, engine, out_ap, in_ap, stream, transpose=False):
        sem, prior, newv = stream.next_slot()
        self.em(engine, lambda e, o=out_ap, i=in_ap, s=sem, p=prior,
                tr=transpose: (
            e.wait_ge(s, p) if p > 0 else None,
            (e.dma_start_transpose(out=o, in_=i) if tr
             else e.dma_start(out=o, in_=i)).then_inc(s, 16)))
        return (sem, newv)

    def xbar(self, engine, out_ap, in_ap, stream):
        return self.dma(engine, out_ap, in_ap, stream, transpose=True)

    def wait(self, engine, *toks):
        toks = _flat(toks)
        if toks:
            self.em(engine, lambda e, tk=toks: [e.wait_ge(s, v) for s, v in tk])

    def run(self, engine, fn, sem):
        prev = self.cnt.get(sem.name, 0)
        t = self._inc(sem, 1)
        if engine in ("vector", "scalar", "gpsimd"):
            self.em(engine, lambda e, f=fn, s=sem, p=prev: (
                e.wait_ge(s, p) if p > 0 else None, f(e).then_inc(s, 1)))
        else:
            self.em(engine, lambda e, f=fn, s=sem: f(e).then_inc(s, 1))
        return t

    def mark(self, engine, sem):
        t = self._inc(sem, 1)
        self.em(engine, lambda e, s=sem: e.sem_inc(s, 1))
        return t


def build_program():
    nc = bass.Bass("TRN2", target_bir_lowering=False)

    x0 = nc.dram_tensor("x0", [T, D], F32, kind="ExternalInput").ap()
    w_in = {n: nc.dram_tensor(f"w_{n}", [D, D], F32, kind="ExternalInput").ap()
            for n in "qkvo"}
    mems = nc.dram_tensor("mems", [4, H, 128, 128], F32, kind="ExternalInput").ap()
    mn_in = nc.dram_tensor("mn", [H * 4, 128], F32, kind="ExternalInput").ap()
    gate_in = nc.dram_tensor("gate", [1, H], F32, kind="ExternalInput").ap()
    rw_in = nc.dram_tensor("r_w", [64, D], F32, kind="ExternalInput").ap()
    rb_in = nc.dram_tensor("r_b", [1, 64], F32, kind="ExternalInput").ap()
    eb_in = nc.dram_tensor("eb", [64, D], F32, kind="ExternalInput").ap()
    identc = nc.dram_tensor("identc", [128, 128], BF16, kind="ExternalInput").ap()
    maskTc = nc.dram_tensor("maskTc", [128, 128], BF16, kind="ExternalInput").ap()
    mgrpc = nc.dram_tensor("mgrpc", [64, 4], BF16, kind="ExternalInput").ap()
    ones1c = nc.dram_tensor("ones1c", [1, 128], BF16, kind="ExternalInput").ap()
    sflagc = nc.dram_tensor("sflagc", [128, 1], F32, kind="ExternalInput").ap()
    rflagc = nc.dram_tensor("rflagc", [128, 1], F32, kind="ExternalInput").ap()

    y_out = nc.dram_tensor("y", [T, D], F32, kind="ExternalOutput").ap()
    if DEBUG:
        dbg_x1 = nc.dram_tensor("dbg_x1", [T, D], F32, kind="ExternalOutput").ap()
        dbg_x2 = nc.dram_tensor("dbg_x2", [T, D], F32, kind="ExternalOutput").ap()
        dbg_st = nc.dram_tensor("dbg_st", [128, 128], F32, kind="ExternalOutput").ap()

    wbf = {n: nc.dram_tensor(f"wbf_{n}", [D, D], BF16).ap() for n in "qkvo"}
    xln_tm = [nc.dram_tensor(f"xln{i}_tm", [T, D], BF16).ap() for i in range(3)]
    kp_hbm = nc.dram_tensor("kp_hbm", [H, T, 128], BF16).ap()
    v_hbm = nc.dram_tensor("v_hbm", [H, T, 128], BF16).ap()
    out_hbm = nc.dram_tensor("out_hbm", [H, T, 128], BF16).ap()
    x1_hbm = nc.dram_tensor("x1_hbm", [T, D], F32).ap()
    x2_hbm = nc.dram_tensor("x2_hbm", [T, D], F32).ap()
    cc_in = [nc.dram_tensor(f"cc_in{h}", [128, 136], F32) for h in range(H)]
    cc_out = [nc.dram_tensor(f"cc_out{h}", [128, 136], F32) for h in range(H)]

    with ExitStack() as ctx:
        E = ctx.enter_context
        bigA = [E(nc.sbuf_tensor(f"bigA{i}", [128, D], BF16)) for i in range(16)]
        bigB = [E(nc.sbuf_tensor(f"bigB{i}", [128, D], BF16)) for i in range(16)]
        R2 = E(nc.sbuf_tensor("R2", [128, 54 * 1024], mybir.dt.uint8))

        off = 0
        xt0, off = carve(nc, R2, off, "xt0", [128, D], F32)
        xt1, off = carve(nc, R2, off, "xt1", [128, D], F32)
        sqd, off = carve(nc, R2, off, "sqd", [128, D], F32)
        xlnst0, off = carve(nc, R2, off, "xlnst0", [128, D], BF16)
        xlnst1, off = carve(nc, R2, off, "xlnst1", [128, D], BF16)
        xlnst = [xlnst0, xlnst1]
        famF_end = off

        offP = 0
        qsb = []
        for i in range(2):
            t_, offP = carve(nc, R2, offP, f"qsb{i}", [128, 512], F32)
            qsb.append(t_)
        etm = []
        for i in range(2):
            t_, offP = carve(nc, R2, offP, f"etm{i}", [128, 1024], F32)
            etm.append(t_)
        kvst = []
        for i in range(2):
            t_, offP = carve(nc, R2, offP, f"kvst{i}", [128, 1024], BF16)
            kvst.append(t_)
        wT = []
        for i in range(16):
            t_, offP = carve(nc, R2, offP, f"wT{i}", [128, 1024], BF16)
            wT.append(t_)
        famP_end = offP

        offA = 0
        Lb = []
        for i in range(2):
            t_, offA = carve(nc, R2, offA, f"Lb{i}", [128, NCH * 129], BF16)
            Lb.append(t_)
        kpsd = []
        for i in range(2):
            t_, offA = carve(nc, R2, offA, f"kpsd{i}", [128, D], BF16)
            kpsd.append(t_)
        vaug = []
        for i in range(2):
            t_, offA = carve(nc, R2, offA, f"vaug{i}", [128, NCH * 132], BF16)
            vaug.append(t_)
        kpds = []
        for i in range(2):
            t_, offA = carve(nc, R2, offA, f"kpds{i}", [128, D], BF16)
            kpds.append(t_)
        outstS, steffS, scbS, ummS = [], [], [], []
        for i in range(2):
            t_, offA = carve(nc, R2, offA, f"outst{i}", [128, D], BF16)
            outstS.append(t_)
            t_, offA = carve(nc, R2, offA, f"steff{i}", [128, 129], BF16)
            steffS.append(t_)
            t_, offA = carve(nc, R2, offA, f"scb{i}", [128, 128], BF16)
            scbS.append(t_)
            t_, offA = carve(nc, R2, offA, f"umm{i}", [128, 4 * 128], F32)
            ummS.append(t_)
        famA_end = offA
        assert max(famP_end, famA_end) <= 54 * 1024, (famP_end, famA_end)

        ident = E(nc.sbuf_tensor("ident", [128, 128], BF16))
        maskT = E(nc.sbuf_tensor("maskT", [128, 128], BF16))
        ones1 = E(nc.sbuf_tensor("ones1", [1, 128], BF16))
        mgrp = E(nc.sbuf_tensor("mgrp", [64, 4], BF16))
        sflag = E(nc.sbuf_tensor("sflag", [128, 1], F32))
        rflag = E(nc.sbuf_tensor("rflag", [128, 1], F32))
        rwT = E(nc.sbuf_tensor("rwT", [128, 16 * 64], BF16))
        rw_bf = E(nc.sbuf_tensor("rw_bf", [64, D], BF16))
        eb_bf = E(nc.sbuf_tensor("eb_bf", [64, D], BF16))
        mn32 = E(nc.sbuf_tensor("mn32", [64, 128], F32))
        mn_bf = E(nc.sbuf_tensor("mn_bf", [64, 128], BF16))
        mnT = E(nc.sbuf_tensor("mnT", [128, 64], BF16))
        rbrow = E(nc.sbuf_tensor("rbrow", [1, 64], F32))
        rbrow_bf = E(nc.sbuf_tensor("rbrow_bf", [1, 64], BF16))
        rb_bc = E(nc.sbuf_tensor("rb_bc", [128, 64], F32))
        grow = E(nc.sbuf_tensor("grow", [1, 16], F32))
        grow_bf = E(nc.sbuf_tensor("grow_bf", [1, 16], BF16))
        g_bc = E(nc.sbuf_tensor("g_bc", [128, 16], F32))
        gm1_bc = E(nc.sbuf_tensor("gm1_bc", [128, 16], F32))
        erow = E(nc.sbuf_tensor("erow", [1, 16], F32))
        eU = E(nc.sbuf_tensor("eU", [1, 4], F32))
        rows_bf = E(nc.sbuf_tensor("rows_bf", [64, 1], BF16))
        relrow = E(nc.sbuf_tensor("relrow", [1, 64], F32))
        wgS = [E(nc.sbuf_tensor(f"wg{i}", [1, 12], F32)) for i in range(2)]
        wg_bfS = [E(nc.sbuf_tensor(f"wg_bf{i}", [1, 4], BF16)) for i in range(2)]
        wbcS = [E(nc.sbuf_tensor(f"wbc{i}", [128, 4], F32)) for i in range(2)]
        memh = [E(nc.sbuf_tensor(f"memh{i}", [128, 512], BF16)) for i in range(2)]
        qsum4 = E(nc.sbuf_tensor("qsum4", [128, 64], F32))
        qsum_bf = E(nc.sbuf_tensor("qsum_bf", [128, 16], BF16))
        qsum_f = E(nc.sbuf_tensor("qsum_f", [128, 16], F32))
        ccsb = E(nc.sbuf_tensor("ccsb", [128, 136], F32))
        ccrvS = [E(nc.sbuf_tensor(f"ccrv{i}", [128, 136], F32)) for i in range(2)]
        mcol = E(nc.sbuf_tensor("mcol", [128, 16], F32))
        rstd0 = E(nc.sbuf_tensor("rstd0", [128, 16], F32))
        negmr0 = E(nc.sbuf_tensor("negmr0", [128, 16], F32))
        rstd1 = E(nc.sbuf_tensor("rstd1", [128, 2], F32))
        negmr1 = E(nc.sbuf_tensor("negmr1", [128, 2], F32))
        rstd2 = E(nc.sbuf_tensor("rstd2", [128, 16], F32))
        negmr2 = E(nc.sbuf_tensor("negmr2", [128, 16], F32))
        sumc = E(nc.sbuf_tensor("sumc", [128, 48], F32))
        sqc = E(nc.sbuf_tensor("sqc", [128, 48], F32))
        colt = E(nc.sbuf_tensor("colt", [128, 8], F32))
        lncol = E(nc.sbuf_tensor("lncol", [128, 96], F32))
        swtc = E(nc.sbuf_tensor("swtc", [128, 16], F32))
        top8 = E(nc.sbuf_tensor("top8", [128, 8], F32))
        lgsb = E(nc.sbuf_tensor("lgsb", [128, 64], F32))
        esb = E(nc.sbuf_tensor("esb", [128, 64], F32))
        e8sb = E(nc.sbuf_tensor("e8sb", [128, 8], F32))
        mskb = E(nc.sbuf_tensor("mskb", [128, 64], F32))
        msbf = E(nc.sbuf_tensor("msbf", [128, 64], BF16))
        msT = E(nc.sbuf_tensor("msT", [64, 16 * 128], BF16))
        dcolS = [E(nc.sbuf_tensor(f"dcol{i}", [128, 4], F32)) for i in range(2)]
        ncolS = [E(nc.sbuf_tensor(f"ncol{i}", [128, 8], F32)) for i in range(2)]

        P = [E(nc.psum_tensor(f"P{i}", [128, 512], F32)) for i in range(7)]
        PT = E(nc.psum_tensor("PT", [128, 1024], BF16))

        SY = Stream([E(nc.semaphore(f"sy{i}")) for i in range(6)])
        SC = Stream([E(nc.semaphore(f"sc{i}")) for i in range(6)])
        GW = Stream([E(nc.semaphore(f"gw{i}")) for i in range(4)])
        GP = Stream([E(nc.semaphore(f"gp{i}")) for i in range(4)])
        CC = Stream([E(nc.semaphore(f"cc{i}")) for i in range(int(os.environ.get("KCC", "4")))], step=1)
        FIN = Stream([E(nc.semaphore(f"fin{i}")) for i in range(3)])
        s_pe = E(nc.semaphore("s_pe"))
        s_dv = E(nc.semaphore("s_dv"))
        s_ac = E(nc.semaphore("s_ac"))

        b = B()

        def barrier():
            waits = (b.val(s_pe), b.val(s_dv), b.val(s_ac), SY.val(),
                     SC.val(), GW.val(), GP.val())
            for eng in B.ENGS:
                b.wait(eng, *waits)

        def ln_stats(xt, rstd_a, negmr_a, ci, scol, dep):
            b.wait("vector", dep)
            b.run("vector", lambda e: e.reduce_sum(
                sumc[:, scol:scol + 1], xt[:, :], axis=AXX), s_dv)
            b.wait("scalar", dep)
            t2 = b.run("scalar", lambda e: e.activation(
                sqd[:, :], xt[:, :], AF.Square, accum_out=sqc[:, scol:scol + 1]),
                s_ac)
            b.wait("vector", t2)
            b.run("vector", lambda e: e.tensor_scalar(
                mcol[:, ci:ci + 1], sumc[:, scol:scol + 1], 1.0 / D, None,
                ALU.mult), s_dv)
            b.run("vector", lambda e: e.tensor_scalar(
                colt[:, 0:1], sqc[:, scol:scol + 1], 1.0 / D, None, ALU.mult),
                s_dv)
            b.run("vector", lambda e: e.tensor_scalar(
                colt[:, 1:2], mcol[:, ci:ci + 1], mcol[:, ci:ci + 1], None,
                ALU.mult), s_dv)
            t3 = b.run("vector", lambda e: e.tensor_scalar(
                colt[:, 0:1], colt[:, 0:1], colt[:, 1:2], EPS_LN,
                ALU.subtract, ALU.add), s_dv)
            b.wait("scalar", t3)
            b.run("scalar", lambda e: e.activation(colt[:, 2:3], colt[:, 0:1],
                                                   AF.Ln), s_ac)
            t5 = b.run("scalar", lambda e: e.activation(
                rstd_a[:, ci:ci + 1], colt[:, 2:3], AF.Exp, scale=-0.5), s_ac)
            b.wait("vector", t5)
            return b.run("vector", lambda e: e.tensor_scalar(
                negmr_a[:, ci:ci + 1], mcol[:, ci:ci + 1], rstd_a[:, ci:ci + 1],
                -1.0, ALU.mult, ALU.mult), s_dv)

        for dst, src in ((ident, identc), (maskT, maskTc), (ones1, ones1c),
                         (mgrp, mgrpc), (sflag, sflagc), (rflag, rflagc),
                         (mn32, mn_in)):
            b.dma("sync", dst[:, :], src[:, :], SY)
        b.dma("sync", grow[0:1, :], gate_in[:, :], SY)
        b.dma("sync", rbrow[0:1, :], rb_in[:, :], SY)
        tconsts = SY.val()
        t_rwbf = b.dma("gpsimd", rw_bf[:, :], rw_in[:, :], GW)
        t_ebbf = b.dma("gpsimd", eb_bf[:, :], eb_in[:, :], GW)
        t_mnbf = b.dma("gpsimd", mn_bf[:, :], mn_in[:, :], GW)

        b.wait("tensor", t_rwbf, tconsts)
        for dt in range(ND):
            tp = b.run("tensor", lambda e, dt=dt: e.transpose(
                PT[:, 0:64], rw_bf[0:64, dt * 128:(dt + 1) * 128],
                ident[0:64, 0:64]),
                s_pe)
            b.wait("vector", tp)
            tv = b.run("vector", lambda e, dt=dt: e.tensor_copy(
                rwT[:, dt * 64:(dt + 1) * 64], PT[:, 0:64]), s_dv)
            b.wait("tensor", tv)
        b.wait("tensor", t_mnbf)
        tp = b.run("tensor", lambda e: e.transpose(PT[:, 64:128], mn_bf[:, :],
                                                   ident[0:64, 0:64]), s_pe)
        b.wait("vector", tp)
        t_mnT = b.run("vector", lambda e: e.tensor_copy(mnT[:, :], PT[:, 64:128]),
                      s_dv)

        b.wait("vector", tconsts)
        t1 = b.run("vector", lambda e: e.tensor_copy(rbrow_bf[:, :],
                                                     rbrow[0:1, :]), s_dv)
        b.wait("tensor", t1)
        t2 = b.run("tensor", lambda e: e.matmul(P[6][:, 0:64], ones1[:, :],
                                                rbrow_bf[:, :], start=True,
                                                stop=True), s_pe)
        b.wait("vector", t2)
        b.run("vector", lambda e: e.tensor_copy(rb_bc[:, :], P[6][:, 0:64]), s_dv)

        b.wait("scalar", tconsts)
        t1 = b.run("scalar", lambda e: e.activation(erow[0:1, 0:16], grow[0:1, :],
                                                    AF.Exp, scale=-1.0), s_ac)
        b.wait("vector", t1)
        b.run("vector", lambda e: e.tensor_scalar(erow[0:1, 0:16],
                                                  erow[0:1, 0:16], 1.0, None,
                                                  ALU.add), s_dv)
        b.run("vector", lambda e: e.reciprocal(grow[0:1, :], erow[0:1, 0:16]),
              s_dv)
        t1 = b.run("vector", lambda e: e.tensor_copy(grow_bf[:, :], grow[0:1, :]),
                   s_dv)
        b.wait("tensor", t1)
        t2 = b.run("tensor", lambda e: e.matmul(P[6][:, 64:80], ones1[:, :],
                                                grow_bf[:, :], start=True,
                                                stop=True), s_pe)
        b.wait("vector", t2)
        b.run("vector", lambda e: e.tensor_copy(g_bc[:, :], P[6][:, 64:80]), s_dv)
        b.run("vector", lambda e: e.tensor_scalar(gm1_bc[:, :], g_bc[:, :], -1.0,
                                                  1.0, ALU.mult, ALU.add), s_dv)

        t1 = b.run("vector", lambda e: e.reduce_sum(colt[0:64, 0:1], mn32[:, :],
                                                    axis=AXX), s_dv)
        b.wait("vector", t1)
        t2 = b.run("vector", lambda e: e.tensor_copy(rows_bf[:, :],
                                                     colt[0:64, 0:1]), s_dv)
        b.wait("tensor", t2)
        t3 = b.run("tensor", lambda e: e.matmul(P[6][0:1, 128:132], rows_bf[:, :],
                                                mgrp[:, :], start=True,
                                                stop=True), s_pe)
        b.wait("vector", t3)
        t4 = b.run("vector", lambda e: e.tensor_scalar(
            erow[0:1, 0:4], P[6][0:1, 128:132], 1.0, -EPS_ATT, ALU.mult,
            ALU.add), s_dv)
        b.wait("scalar", t4)
        t5 = b.run("scalar", lambda e: e.sign(erow[0:1, 4:8], erow[0:1, 0:4]),
                   s_ac)
        b.wait("vector", t5)
        b.run("vector", lambda e: e.tensor_scalar(erow[0:1, 4:8], erow[0:1, 4:8],
                                                  -1.0, 0.0, ALU.mult, ALU.max),
              s_dv)
        b.run("vector", lambda e: e.tensor_scalar(eU[0:1, :], erow[0:1, 4:8],
                                                  -1.0, 1.0, ALU.mult, ALU.add),
              s_dv)
        for z in (qsum4, sqc, sumc, ccsb):
            b.run("vector", lambda e, z=z: e.memset(z[:, :], 0.0), s_dv)

        class _Cut(Exception):
            pass

        def cut(k):
            if PH <= k:
                dump_dbg()
                junk_finish()
                raise _Cut()

        def dump_dbg():
            if not DEBUG:
                return
            b.wait("gpsimd", b.val(s_dv), b.val(s_ac), b.val(s_pe),
                   SY.val(), SC.val())
            b.dma("gpsimd", dbg_st[:, 0:64], lgsb[:, :], GP)
            b.dma("gpsimd", dbg_st[:, 64:72], top8[:, :], GP)
            b.dma("gpsimd", dbg_st[:, 72:88], swtc[:, :], GP)
            b.dma("gpsimd", dbg_st[:, 88:104], esb[:, 0:16], GP)
            b.dma("gpsimd", dbg_st[:, 104:120], rstd0[:, :], GP)
            if PH == 3:
                b.dma("gpsimd", dbg_x2[0:64, :], msT[:, :], GP)
                b.dma("gpsimd", dbg_x2[128:256, :], bigA[0][:, :], GP)
                b.dma("gpsimd", dbg_x2[256:384, :], xln_tm[0][1792:1920, :], GP)
            if PH == 7:
                b.dma("gpsimd", dbg_st[0:1, 0:64], relrow[0:1, :], GP)
                b.dma("gpsimd", dbg_st[0:128, 64:80], g_bc[:, :], GP)
                b.dma("gpsimd", dbg_st[0:128, 80:84], wbc[:, :], GP)
                b.dma("gpsimd", dbg_st[0:128, 84:92], ncol[:, :], GP)
            if PH in (5, 7):
                src_d = out_hbm if PH == 7 else kp_hbm
                for hh_ in range(16):
                    b.dma("gpsimd",
                          dbg_x1[hh_ * 128:(hh_ + 1) * 128, :].rearrange(
                              "p (n e) -> p n e", e=128),
                          v_hbm[hh_][:, :].rearrange("(n p) e -> p n e",
                                                     p=128), GP)
                for hh_ in range(16):
                    b.dma("gpsimd",
                          dbg_x2[hh_ * 128:(hh_ + 1) * 128, :].rearrange(
                              "p (n e) -> p n e", e=128),
                          src_d[hh_][:, :].rearrange("(n p) e -> p n e",
                                                     p=128), GP)
            b.wait("gpsimd", GP.val())

        def junk_finish():
            toks = []
            for tt in range(NT):
                toks.append(b.dma("sync", y_out[tt * 128:(tt + 1) * 128, :],
                                  x0[tt * 128:(tt + 1) * 128, :], FIN))
            b.wait("sync", toks[-1])
            b.wait("scalar", toks[-1])
            b.wait("gpsimd", toks[-1])
            b.wait("vector", toks[-1])
            b.wait("tensor", toks[-1])

        try:
            xtl = [xt0, xt1]
            ld = [None] * NT
            ld[0] = b.dma("sync", xtl[0][:, :], x0[0:128, :], SY)
            ld[1] = b.dma("sync", xtl[1][:, :], x0[128:256, :], SY)
            prev_st = [None, None]
            gate_tok = None
            for tt in range(NT):
                xt = xtl[tt % 2]
                t6 = ln_stats(xt, rstd0, negmr0, tt, tt, ld[tt])
                b.wait("vector", prev_st[tt % 2], t6)
                tcp = b.run("vector", lambda e, tt=tt, xt=xt: e.tensor_scalar(
                    xlnst[tt % 2][:, :], xt[:, :], rstd0[:, tt:tt + 1],
                    negmr0[:, tt:tt + 1], ALU.mult, ALU.add), s_dv)
                b.wait("scalar", tcp)
                st = b.dma("scalar", xln_tm[0][tt * 128:(tt + 1) * 128, :],
                           xlnst[tt % 2][:, :], SC)
                if tt == 0:
                    gate_tok = st
                prev_st[tt % 2] = st
                if tt + 2 < NT:
                    b.wait("sync", tcp)
                    ld[tt + 2] = b.dma("sync", xt[:, :],
                                       x0[(tt + 2) * 128:(tt + 3) * 128, :], SY)
            b.wait("gpsimd", gate_tok)
            wtok = {n: b.dma("gpsimd", wbf[n][:, :], w_in[n][:, :], GW)
                    for n in "qkvo"}
            t_xln0_done = SC.val()
            cut(1)

            xb = [None] * ND
            for dt in range(ND):
                b.wait("scalar", t_xln0_done)
                xb[dt] = b.xbar("scalar", bigA[dt][:, :],
                                xln_tm[0][:, dt * 128:(dt + 1) * 128], SC)
            b.wait("tensor", t_ebbf)
            for tt in range(NT):
                for dt in range(ND):
                    if tt == 0:
                        b.wait("tensor", xb[dt])
                    t_router = b.run("tensor", lambda e, tt=tt, dt=dt: e.matmul(
                        P[tt // 8][:, (tt % 8) * 64:(tt % 8) * 64 + 64],
                        bigA[dt][:, tt * 128:(tt + 1) * 128],
                        rwT[:, dt * 64:(dt + 1) * 64],
                        start=(dt == 0), stop=(dt == ND - 1)), s_pe)
            cut(2)

            state = {"prev_tail_act": None, "prev_tail_dve": None,
                     "prev_xt1_free": None, "prev_st": [None, None],
                     "prev_adds": None}

            def ffn_tail(tt, rstd_a, negmr_a, ci, phase2, xln_dst, x_dst, scol,
                         dbg_dst):
                b.wait("vector", state["prev_tail_act"], t_router,
                       b.val(s_pe), b.val(s_ac))
                b.wait("scalar", state["prev_tail_dve"], b.val(s_dv),
                       b.val(s_pe))
                b.wait("tensor", b.val(s_dv), b.val(s_ac))
                if KSUB == 10:
                    b.run("vector", lambda e: e.tensor_tensor(
                        lgsb[:, :], rb_bc[:, :], rb_bc[:, :], ALU.add), s_dv)
                    return None
                b.run("vector", lambda e, tt=tt: e.tensor_tensor(
                    lgsb[:, :], P[tt // 8][:, (tt % 8) * 64:(tt % 8) * 64 + 64],
                    rb_bc[:, :], ALU.add), s_dv)
                if KSUB == 11:
                    return None
                b.run("vector", lambda e: e.max(out=top8[:, :], in_=lgsb[:, :]), s_dv)
                t1 = b.run("vector", lambda e: e.tensor_scalar(
                    colt[:, 3:4], top8[:, 0:1], -1.0, None, ALU.mult), s_dv)
                if KSUB == 12:
                    return None
                b.wait("scalar", t1)
                t2 = b.run("scalar", lambda e: e.activation(
                    esb[:, :], lgsb[:, :], AF.Exp, bias=colt[:, 3:4]), s_ac)
                t3 = b.run("scalar", lambda e: e.activation(
                    e8sb[:, :], top8[:, :], AF.Exp, bias=colt[:, 3:4]), s_ac)
                b.wait("vector", t2, t3)
                if KSUB == 13:
                    return None
                b.run("vector", lambda e: e.reduce_sum(colt[:, 4:5], esb[:, :],
                                                       axis=AXX), s_dv)
                b.run("vector", lambda e: e.reduce_sum(colt[:, 5:6], e8sb[:, :],
                                                       axis=AXX), s_dv)
                b.run("vector", lambda e: e.reciprocal(colt[:, 6:7], colt[:, 4:5]),
                      s_dv)
                b.run("vector", lambda e: e.tensor_scalar(
                    colt[:, 6:7], colt[:, 6:7], 0.5, None, ALU.mult), s_dv)
                b.run("vector", lambda e, tt=tt: e.tensor_scalar(
                    swtc[:, tt:tt + 1], colt[:, 5:6], colt[:, 6:7], None, ALU.mult),
                    s_dv)
                if KSUB == 14:
                    return None
                b.run("vector", lambda e: e.tensor_scalar(
                    mskb[:, :], lgsb[:, :], top8[:, 7:8], None, ALU.is_ge), s_dv)
                t4 = b.run("vector", lambda e: e.scalar_tensor_tensor(
                    msbf[:, :], esb[:, :], colt[:, 6:7], mskb[:, :], ALU.mult,
                    ALU.mult), s_dv)
                state["prev_tail_act"] = t3
                state["prev_tail_dve"] = t4
                if KSUB < 2:
                    return None
                b.wait("tensor", t4)
                t5 = b.run("tensor", lambda e: e.transpose(
                    PT[0:64, 128:256], msbf[:, :], ident[:, :]), s_pe)
                b.wait("vector", t5)
                t6 = b.run("vector", lambda e, tt=tt: e.tensor_copy(
                    msT[:, tt * 128:(tt + 1) * 128], PT[0:64, 128:256]), s_dv)
                b.wait("tensor", t6, state.get("prev_adds"))
                if KSUB < 3:
                    return None
                mm = []
                for ec in range(4):
                    mm.append(b.run("tensor", lambda e, tt=tt, ec=ec: e.matmul(
                        P[2 + ec][:, :], msT[:, tt * 128:(tt + 1) * 128],
                        eb_bf[:, ec * 512:(ec + 1) * 512], start=True, stop=True),
                        s_pe))
                if KSUB < 4:
                    return None
                b.run("vector", lambda e, tt=tt: e.tensor_scalar(
                    colt[:, 7:8], swtc[:, tt:tt + 1], rstd_a[:, ci:ci + 1], None,
                    ALU.mult), s_dv)
                t7 = b.run("vector", lambda e: e.tensor_scalar(
                    colt[:, 7:8], colt[:, 7:8], 1.0, None, ALU.add), s_dv)
                t8 = b.run("vector", lambda e, tt=tt: e.tensor_scalar(
                    colt[:, 2:3], swtc[:, tt:tt + 1], negmr_a[:, ci:ci + 1], None,
                    ALU.mult), s_dv)
                b.wait("vector", t7, t8, state["prev_xt1_free"],
                       state.get("prev_xt1_b"))
                t9 = b.run("vector", lambda e: e.tensor_scalar(
                    xt1[:, :], xt0[:, :], colt[:, 7:8], colt[:, 2:3],
                    ALU.mult, ALU.add), s_dv)
                b.wait("vector", *mm)
                adds = []
                for ec in range(4):
                    adds.append(b.run("vector", lambda e, ec=ec: e.tensor_tensor(
                        xt1[:, ec * 512:(ec + 1) * 512],
                        xt1[:, ec * 512:(ec + 1) * 512], P[2 + ec][:, :], ALU.add),
                        s_dv))
                state["prev_tail_act"] = t9
                state["prev_tail_dve"] = adds[-1]
                state["prev_adds"] = adds[-1]
                if KSUB < 5:
                    state["prev_tail_dve"] = adds[-1]
                    return None
                if phase2:
                    b.wait("scalar", *adds)
                    st = b.dma("scalar", y_out[tt * 128:(tt + 1) * 128, :],
                               xt1[:, :], FIN)
                    state["prev_xt1_free"] = st
                    return t9
                tln = ln_stats(xt1, rstd1, negmr1, tt % 2, scol, adds[-1])
                b.wait("vector", tln, state["prev_st"][tt % 2])
                tcp = b.run("vector", lambda e, tt=tt: e.tensor_scalar(
                    xlnst[tt % 2][:, :], xt1[:, :],
                    rstd1[:, (tt % 2):(tt % 2) + 1],
                    negmr1[:, (tt % 2):(tt % 2) + 1], ALU.mult, ALU.add), s_dv)
                b.wait("scalar", tcp)
                st1 = b.dma("scalar", xln_dst[tt * 128:(tt + 1) * 128, :],
                            xlnst[tt % 2][:, :], SC)
                st2 = b.dma("scalar", x_dst[tt * 128:(tt + 1) * 128, :], xt1[:, :],
                            SC)
                if dbg_dst is not None:
                    st2 = b.dma("scalar", dbg_dst[tt * 128:(tt + 1) * 128, :],
                                xt1[:, :], FIN)
                    b.wait("gpsimd", tcp)
                    st2b = b.dma("gpsimd", dbg_x2[tt * 128:(tt + 1) * 128, :],
                                 xt1[:, :], GP)
                    state["prev_xt1_b"] = st2b
                state["prev_st"][tt % 2] = st1
                state["prev_xt1_free"] = st2
                state["prev_tail_act"] = tcp
                return tcp

            b.wait("sync", b.val(s_dv), b.val(s_ac))
            prev_ld_reader = None
            for tt in range(NT):
                b.wait("sync", prev_ld_reader)
                ldx = b.dma("sync", xt0[:, :], x0[tt * 128:(tt + 1) * 128, :], SY)
                b.wait("vector", ldx)
                b.wait("scalar", ldx)
                prev_ld_reader = ffn_tail(
                    tt, rstd0, negmr0, tt, False, xln_tm[1], x1_hbm, 16 + tt,
                    dbg_x1 if DEBUG else None)

            cut(3)
            barrier()

            xb1 = [None] * ND
            for dt in range(ND):
                xb1[dt] = b.xbar("scalar", bigA[dt][:, :],
                                 xln_tm[1][:, dt * 128:(dt + 1) * 128], SC)

            def stream_wT(name, half):
                toks = []
                dep = b.val(s_pe)
                for dt in range(ND):
                    b.wait("scalar", wtok[name], dep)
                    toks.append(b.xbar("scalar", wT[dt][:, :],
                                       wbf[name][half * 1024:(half + 1) * 1024,
                                                 dt * 128:(dt + 1) * 128], SC))
                return toks

            prev_stt = [None, None]
            for half in range(2):
                wts = stream_wT("q", half)
                prev_evac = {}
                for ei in range(8):
                    et = half * 8 + ei
                    for pair in range(2):
                        bk = (P[0], P[1]) if pair == 0 else (P[2], P[3])
                        b.wait("tensor", prev_evac.get(pair * 2),
                               prev_evac.get(pair * 2 + 1))
                        for dt in range(ND):
                            if ei == 0 and pair == 0:
                                b.wait("tensor", wts[dt], xb1[dt])
                            for tci in range(2):
                                tc = pair * 2 + tci
                                tmm_t = b.run("tensor", lambda e, et=et, dt=dt,
                                              tc=tc, bb=bk[tci]: e.matmul(
                                          bb[:, :],
                                          wT[dt][:, (et % 8) * 128:
                                                 (et % 8) * 128 + 128],
                                          bigA[dt][:, tc * 512:(tc + 1) * 512],
                                          start=(dt == 0), stop=(dt == ND - 1)),
                                      s_pe)
                        for tci in range(2):
                            tc = pair * 2 + tci
                            bb = bk[tci]
                            p = tc % 2
                            b.wait("scalar", tmm_t, prev_stt[p])
                            ta = b.run("scalar", lambda e, bb=bb, p=p, et=et,
                                       tc=tc:
                                  e.activation(qsb[p][:, :], bb[:, :], AF.Copy,
                                               accum_out=qsum4[:, et * 4 + tc:
                                                               et * 4 + tc + 1]),
                                  s_ac)
                            b.wait("vector", ta)
                            tb = b.run("vector", lambda e, p=p: e.tensor_scalar(
                                etm[p][:, 0:512], qsb[p][:, :], 0.0, None, ALU.min),
                                s_dv)
                            b.wait("scalar", tb)
                            tc2 = b.run("scalar", lambda e, p=p: e.activation(
                                etm[p][:, 512:1024], etm[p][:, 0:512], AF.Exp),
                                s_ac)
                            b.wait("vector", tc2)
                            td = b.run("vector", lambda e, p=p, et=et, tc=tc:
                                       e.scalar_tensor_tensor(
                                           bigB[et][:, tc * 512:(tc + 1) * 512],
                                           qsb[p][:, :], 0.0, etm[p][:, 512:1024],
                                           ALU.max, ALU.add), s_dv)
                            prev_evac[tc] = td
                            prev_stt[p] = td

            t1 = b.run("vector", lambda e: e.reduce_sum(
                qsum_f[:, :], qsum4[:, :].rearrange("p (h f) -> p h f", f=4),
                axis=AXX), s_dv)
            b.wait("vector", t1)
            t_qs = b.run("vector", lambda e: e.tensor_copy(qsum_bf[:, :],
                                                           qsum_f[:, :]), s_dv)
            b.wait("tensor", t_qs, t_mnT)
            for h in range(H):
                t_rel = b.run("tensor", lambda e, h=h: e.matmul(
                    P[6][0:1, 256 + h * 4:256 + h * 4 + 4], qsum_bf[:, h:h + 1],
                    mnT[:, h * 4:h * 4 + 4], start=True, stop=True), s_pe)
            b.wait("vector", t_rel)
            t_relv = b.run("vector", lambda e: e.tensor_scalar(
                relrow[0:1, :], P[6][0:1, 256:320], SCALE / 4096.0, None, ALU.mult),
                s_dv)

            cut(4)
            def proj_b(name, dest_hbm, do_epo):
                prev_sp_slot = [None, None]
                for half in range(2):
                    wts = stream_wT(name, half)
                    prev_ev = [None, None]
                    for tt in range(NT):
                        ps_ = tt % 2
                        bk = (P[0], P[1]) if ps_ == 0 else (P[2], P[3])
                        b.wait("tensor", prev_ev[ps_])
                        for dt in range(ND):
                            if tt == 0:
                                b.wait("tensor", wts[dt], xb1[dt])
                            for eci in range(2):
                                tmm_t = b.run("tensor", lambda e, tt=tt, dt=dt,
                                              eci=eci, bb=bk[eci]: e.matmul(
                                          bb[:, :],
                                          bigA[dt][:, tt * 128:(tt + 1) * 128],
                                          wT[dt][:, eci * 512:(eci + 1) * 512],
                                          start=(dt == 0), stop=(dt == ND - 1)),
                                      s_pe)
                        last = None
                        for eci in range(2):
                            bb = bk[eci]
                            p = tt % 2
                            if do_epo:
                                b.wait("vector", tmm_t, prev_sp_slot[p])
                                tb = b.run("vector", lambda e, bb=bb, p=p:
                                           e.tensor_scalar(etm[p][:, 0:512],
                                                           bb[:, :], 0.0, None,
                                                           ALU.min), s_dv)
                                b.wait("scalar", tb)
                                tc2 = b.run("scalar", lambda e, p=p: e.activation(
                                    etm[p][:, 512:1024], etm[p][:, 0:512], AF.Exp),
                                    s_ac)
                                b.wait("vector", tc2)
                                last = b.run(
                                    "vector", lambda e, bb=bb, p=p, eci=eci:
                                    e.scalar_tensor_tensor(
                                        kvst[p][:, eci * 512:(eci + 1) * 512],
                                        bb[:, :], 0.0, etm[p][:, 512:1024],
                                        ALU.max, ALU.add), s_dv)
                            else:
                                b.wait("vector", tmm_t, prev_sp_slot[p])
                                last = b.run("vector", lambda e, bb=bb, p=p,
                                             eci=eci: e.tensor_copy(
                                                 kvst[p][:, eci * 512:
                                                         (eci + 1) * 512],
                                                 bb[:, :]), s_dv)
                            prev_ev[ps_] = last
                        b.wait("sync", last)
                        prev_sp_slot[tt % 2] = b.dma(
                            "sync",
                            dest_hbm[half * 8:(half + 1) * 8,
                                     tt * 128:(tt + 1) * 128, :].rearrange(
                                         "h p e -> p h e"),
                            kvst[tt % 2][:, :].rearrange("p (h e) -> p h e", e=128),
                            SY)
                return prev_sp_slot

            proj_b("k", kp_hbm, True)
            proj_b("v", v_hbm, False)

            cut(5)
            barrier()

            for i in range(2):
                b.run("vector", lambda e, i=i: e.memset(
                    vaug[i][:, :].rearrange("p (n c) -> p n c", c=132)[:, :, 128:129],
                    1.0), s_dv)
            t_vone = b.val(s_dv)

            attn_state = {"prev_Lcopy": None}

            def emit_A(h, dep_tensor, dep_vec):
                sl = h % 2
                b.wait("sync", dep_tensor)
                tk = b.dma("sync",
                           kpsd[sl][:, :].rearrange("p (n e) -> p n e", e=128),
                           kp_hbm[h][:, :].rearrange("(n p) e -> p n e", p=128),
                           SY)
                tv = b.dma("sync",
                           vaug[sl][:, :].rearrange("p (n c) -> p n c",
                                                    c=132)[:, :, 0:128],
                           v_hbm[h][:, :].rearrange("(n p) e -> p n e", p=128),
                           SY)
                b.wait("tensor", tk, tv, t_vone)
                b.wait("scalar", dep_vec)
                tcopy = attn_state["prev_Lcopy"]
                for j in range(NCH):
                    if tcopy is not None:
                        b.wait("tensor", tcopy)
                    tj = b.run("tensor", lambda e, sl=sl, j=j: e.matmul(
                        P[5][:, 0:129], kpsd[sl][:, j * 128:(j + 1) * 128],
                        vaug[sl][:, j * 132:j * 132 + 129],
                        start=(j == 0), stop=(j == NCH - 1)), s_pe)
                    b.wait("scalar", tj)
                    tcopy = b.run("scalar", lambda e, sl=sl, j=j: e.activation(
                        Lb[sl][:, j * 129:(j + 1) * 129], P[5][:, 0:129],
                        AF.Copy), s_ac)
                attn_state["prev_Lcopy"] = tcopy
                b.wait("vector", GP.val(), tcopy)
                t1 = b.run("vector", lambda e, sl=sl: e.tensor_scalar(
                    ccsb[:, 0:129], Lb[sl][:, 15 * 129:16 * 129], sflag[:, 0:1],
                    None, ALU.mult), s_dv)
                b.wait("vector", t_relv)
                t2 = b.run("vector", lambda e, h=h: e.tensor_copy(
                    ccsb[0:1, 129:133], relrow[0:1, h * 4:h * 4 + 4]), s_dv)
                b.wait("gpsimd", t1, t2)
                b.dma("gpsimd", cc_in[h].ap(), ccsb[:, :], GP)
                sem, prior, newv = CC.next_slot()
                b.em("gpsimd", lambda e, h=h, s=sem, p=prior: (
                    e.wait_ge(s, p) if p > 0 else None,
                    e.collective_compute(
                        "AllReduce", ALU.add,
                        replica_groups=[[0, 1], [2, 3], [4, 5], [6, 7]],
                        ins=[cc_in[h].ap().opt()], outs=[cc_out[h].ap().opt()]
                    ).then_inc(s, 1)))
                return (sem, newv)

            def emit_B_hdr(h, st, tcc, dep_dve, dep_sp):
                sl = h % 2
                b.wait("vector", b.val(s_ac), b.val(s_pe), SC.val(), dep_sp)
                b.wait("scalar", b.val(s_dv))
                b.wait("gpsimd", b.val(s_pe))
                tmh = b.dma("gpsimd", memh[sl][:, :].rearrange(
                    "p (m e) -> p m e", e=128),
                    mems[:, h].rearrange("m p e -> p m e"), GW)
                b.wait("sync", b.val(s_pe))
                tds = b.xbar("sync", kpds[sl][:, :], kp_hbm[h][:, :], SY)
                b.wait("gpsimd", tcc, dep_dve)
                ccrv, wg, wg_bf, wbc = ccrvS[st], wgS[st], wg_bfS[st], wbcS[st]
                wcol = 320 + st * 8
                trv = b.dma("gpsimd", ccrv[:, :], cc_out[h].ap(), GP)
                b.wait("vector", trv)
                tse = b.run("vector", lambda e, st=st: e.tensor_scalar(
                    steffS[st][:, :], ccrvS[st][:, 0:129], rflag[:, 0:1], None,
                    ALU.mult), s_dv)
                b.run("vector", lambda e: e.tensor_tensor(
                    wg[0:1, 0:4], ccrv[0:1, 129:133], eU[0:1, :], ALU.mult), s_dv)
                b.run("vector", lambda e: e.scalar_tensor_tensor(
                    wg[0:1, 0:4], erow[0:1, 4:8], -1e30, wg[0:1, 0:4], ALU.mult,
                    ALU.add), s_dv)
                b.run("vector", lambda e: e.reduce_max(wg[0:1, 4:5], wg[0:1, 0:4],
                                                       axis=AXX), s_dv)
                t1 = b.run("vector", lambda e: e.tensor_scalar(
                    wg[0:1, 5:6], wg[0:1, 4:5], -1.0, None, ALU.mult), s_dv)
                b.wait("scalar", t1)
                t2 = b.run("scalar", lambda e: e.activation(
                    wg[0:1, 0:4], wg[0:1, 0:4], AF.Exp, bias=wg[0:1, 5:6]), s_ac)
                b.wait("vector", t2)
                b.run("vector", lambda e: e.reduce_sum(wg[0:1, 6:7], wg[0:1, 0:4],
                                                       axis=AXX), s_dv)
                b.run("vector", lambda e: e.reciprocal(wg[0:1, 7:8], wg[0:1, 6:7]),
                      s_dv)
                b.run("vector", lambda e: e.tensor_scalar(
                    wg[0:1, 0:4], wg[0:1, 0:4], wg[0:1, 7:8], None, ALU.mult), s_dv)
                b.run("vector", lambda e: e.tensor_tensor(
                    wg[0:1, 0:4], wg[0:1, 0:4], eU[0:1, :], ALU.mult), s_dv)
                b.run("vector", lambda e, h=h: e.tensor_scalar(
                    wg[0:1, 0:4], wg[0:1, 0:4], g_bc[0:1, h:h + 1], None, ALU.mult),
                    s_dv)
                t3 = b.run("vector", lambda e: e.tensor_copy(wg_bf[:, :],
                                                             wg[0:1, 0:4]), s_dv)
                b.wait("tensor", t3)
                t4 = b.run("tensor", lambda e, wcol=wcol: e.matmul(
                    P[6][:, wcol:wcol + 4], ones1[:, :], wg_bf[:, :], start=True,
                    stop=True), s_pe)
                b.wait("vector", t4)
                t_wbc = b.run("vector", lambda e, wcol=wcol: e.tensor_copy(
                    wbc[:, :], P[6][:, wcol:wcol + 4]), s_dv)
                return dict(h=h, st=st, sl=sl, tse=tse, tds=tds, tmh=tmh,
                            t_wbc=t_wbc, pa_free=None, umm_free=None,
                            pnf=[None, None], last_dve=None,
                            last_tintra=None)

            def emit_B_j(C, j):
                h, st, sl = C["h"], C["st"], C["sl"]
                steff, scb = steffS[st], scbS[st]
                umm, outst = ummS[st], outstS[st]
                dcol, ncol, wbc = dcolS[st], ncolS[st], wbcS[st]
                PA = P[4] if st == 0 else P[3]
                Pn = P[st]
                cj = (j % 2) * 256
                sc = st * 256 + (j % 2) * 128
                nsl = 384 + st * 32 + (j % 2) * 8
                b.wait("tensor", C["pnf"][j % 2])
                if j == 0:
                    b.wait("tensor", C["tds"], C["tmh"], C["tse"])
                tsc = b.run("tensor", lambda e, h=h, j=j, sl=sl, sc=sc: e.matmul(
                    P[2][:, sc:sc + 128],
                    kpds[sl][:, j * 128:(j + 1) * 128],
                    bigB[h][:, j * 128:(j + 1) * 128], start=True, stop=True),
                    s_pe)
                b.run("tensor", lambda e, h=h, j=j, cj=cj, Pn=Pn, steff=steff:
                      e.matmul(
                    Pn[:, cj:cj + 129], bigB[h][:, j * 128:(j + 1) * 128],
                    steff[:, :], start=True, stop=False), s_pe)
                if j > 0:
                    b.run("tensor", lambda e, h=h, j=j, cj=cj, Pn=Pn, sl=sl:
                          e.matmul(
                        Pn[:, cj:cj + 129], bigB[h][:, j * 128:(j + 1) * 128],
                        Lb[sl][:, (j - 1) * 129:j * 129], start=False,
                        stop=False), s_pe)
                b.wait("tensor", C["pa_free"])
                tpa = b.run("tensor", lambda e, h=h, j=j, sl=sl, PA=PA: e.matmul(
                    PA[:, 0:512], bigB[h][:, j * 128:(j + 1) * 128],
                    memh[sl][:, 0:512], start=True, stop=True), s_pe)
                b.run("tensor", lambda e, h=h, j=j, nsl=nsl: e.matmul(
                    P[6][:, nsl:nsl + 4], bigB[h][:, j * 128:(j + 1) * 128],
                    mnT[:, h * 4:h * 4 + 4], start=True, stop=True), s_pe)
                b.wait("vector", tsc)
                tm1 = b.run("vector", lambda e, sc=sc, scb=scb: e.tensor_tensor(
                    scb[:, :], P[2][:, sc:sc + 128], maskT[:, :], ALU.mult),
                    s_dv)
                b.wait("tensor", tm1)
                tintra = b.run("tensor", lambda e, j=j, sl=sl, cj=cj, Pn=Pn,
                               scb=scb: e.matmul(
                    Pn[:, cj:cj + 129], scb[:, :],
                    vaug[sl][:, j * 132:j * 132 + 129], start=False, stop=True),
                    s_pe)
                C["last_tintra"] = tintra
                b.wait("vector", tintra, C["t_wbc"])
                b.run("vector", lambda e, cj=cj, Pn=Pn, dcol=dcol: e.tensor_scalar(
                    dcol[:, 0:1], Pn[:, cj + 128:cj + 129], EPS_ATT, None, ALU.max),
                    s_dv)
                b.run("vector", lambda e, dcol=dcol: e.reciprocal(
                    dcol[:, 1:2], dcol[:, 0:1]), s_dv)
                b.run("vector", lambda e, h=h, dcol=dcol: e.tensor_scalar(
                    dcol[:, 2:3], dcol[:, 1:2], gm1_bc[:, h:h + 1], None,
                    ALU.mult), s_dv)
                b.run("vector", lambda e, nsl=nsl, ncol=ncol: e.tensor_scalar(
                    ncol[:, 0:4], P[6][:, nsl:nsl + 4], EPS_ATT, None, ALU.max),
                    s_dv)
                b.run("vector", lambda e, ncol=ncol: e.reciprocal(
                    ncol[:, 4:8], ncol[:, 0:4]), s_dv)
                tnc = b.run("vector", lambda e, ncol=ncol, wbc=wbc: e.tensor_tensor(
                    ncol[:, 0:4], ncol[:, 4:8], wbc[:, :], ALU.mult), s_dv)
                b.wait("scalar", tpa, tnc, C["umm_free"])
                ta0 = b.run("scalar", lambda e, PA=PA, ncol=ncol, umm=umm:
                            e.activation(umm[:, 0:128], PA[:, 0:128], AF.Copy,
                                         scale=ncol[:, 0:1]), s_ac)
                ta2 = b.run("scalar", lambda e, PA=PA, ncol=ncol, umm=umm:
                            e.activation(umm[:, 128:256], PA[:, 256:384],
                                         AF.Copy, scale=ncol[:, 2:3]), s_ac)
                b.wait("vector", ta0, ta2)
                b.run("vector", lambda e, PA=PA, ncol=ncol, umm=umm:
                      e.scalar_tensor_tensor(
                          umm[:, 256:384], PA[:, 128:256], ncol[:, 1:2],
                          umm[:, 0:128], ALU.mult, ALU.add), s_dv)
                td3 = b.run("vector", lambda e, PA=PA, ncol=ncol, umm=umm:
                            e.scalar_tensor_tensor(
                                umm[:, 384:512], PA[:, 384:512], ncol[:, 3:4],
                                umm[:, 128:256], ALU.mult, ALU.add), s_dv)
                b.run("vector", lambda e, cj=cj, Pn=Pn,
                      dcol=dcol, umm=umm: e.scalar_tensor_tensor(
                          umm[:, 0:128],
                          Pn[:, cj:cj + 128], dcol[:, 2:3], umm[:, 256:384],
                          ALU.mult, ALU.add), s_dv)
                tf2 = b.run("vector", lambda e, j=j, outst=outst, umm=umm:
                            e.tensor_tensor(
                                outst[:, j * 128:(j + 1) * 128],
                                umm[:, 0:128],
                                umm[:, 384:512], ALU.add), s_dv)
                C["pa_free"] = td3
                C["umm_free"] = tf2
                C["pnf"][j % 2] = tf2
                C["last_dve"] = tf2

            def emit_B_tail(C):
                h, st = C["h"], C["st"]
                b.wait("sync", C["last_dve"])
                tsp = b.dma("sync", out_hbm[h][:, :].rearrange(
                    "(n p) e -> p n e", p=128),
                    outstS[st][:, :].rearrange("p (n e) -> p n e", e=128), SY)
                b.wait("sync", tsp)
                txb = b.xbar("sync", bigB[h][:, :], out_hbm[h][:, :], SY)
                C["tail_sp"] = tsp
                return txb

            tccs = [None] * H
            xb_out = [None] * H
            tccs[0] = emit_A(0, None, None)
            tccs[1] = emit_A(1, None, None)
            prev_dve = [None, None]
            prev_sp = [None, None]
            for hp in range(0, H, 2):
                CA = emit_B_hdr(hp, 0, tccs[hp], prev_dve[0], prev_sp[0])
                CB = emit_B_hdr(hp + 1, 1, tccs[hp + 1], prev_dve[1], prev_sp[1])
                for j in range(NCH):
                    emit_B_j(CA, j)
                    emit_B_j(CB, j)
                xb_out[hp] = emit_B_tail(CA)
                xb_out[hp + 1] = emit_B_tail(CB)
                prev_dve = [CA["last_dve"], CB["last_dve"]]
                prev_sp = [CA["tail_sp"], CB["tail_sp"]]
                if hp + 2 < H:
                    tccs[hp + 2] = emit_A(hp + 2, CA["last_tintra"],
                                          CA["last_dve"])
                if hp + 3 < H:
                    tccs[hp + 3] = emit_A(hp + 3, CB["last_tintra"],
                                          CB["last_dve"])

            cut(6)
            barrier()

            xbo = [None] * ND
            for dt in range(ND):
                eng, s = (("sync", SY) if dt % 2 == 0 else ("scalar", SC))
                b.wait(eng, wtok["o"])
                xbo[dt] = b.xbar(eng, bigA[dt][:, :],
                                 wbf["o"][:, dt * 128:(dt + 1) * 128], s)
            prev_ev = [None, None, None, None]
            prev_rd = None
            prev_sp2 = [None, None]
            for tt in range(NT):
                b.wait("scalar", prev_rd)
                ldx1 = b.dma("scalar", xt0[:, :],
                             x1_hbm[tt * 128:(tt + 1) * 128, :], SC)
                tadds = []
                for ec in range(4):
                    b.wait("tensor", prev_ev[ec])
                    for hh in range(H):
                        if tt == 0 and ec == 0:
                            b.wait("tensor", xbo[hh], xb_out[hh])
                        tg = b.run("tensor", lambda e, tt=tt, hh=hh, ec=ec:
                                   e.matmul(
                            P[ec][:, :], bigB[hh][:, tt * 128:(tt + 1) * 128],
                            bigA[hh][:, ec * 512:(ec + 1) * 512],
                            start=(hh == 0), stop=(hh == H - 1)), s_pe)
                    b.wait("vector", tg, ldx1 if ec == 0 else None,
                           state["prev_xt1_free"] if ec == 0 else None)
                    ta = b.run("vector", lambda e, ec=ec: e.tensor_tensor(
                        xt1[:, ec * 512:(ec + 1) * 512],
                        xt0[:, ec * 512:(ec + 1) * 512], P[ec][:, :], ALU.add),
                        s_dv)
                    tadds.append(ta)
                    prev_ev[ec] = ta
                prev_rd = tadds[-1]
                tln = ln_stats(xt1, rstd2, negmr2, tt, 32 + tt, tadds[-1])
                b.wait("vector", tln, prev_sp2[tt % 2])
                tcp = b.run("vector", lambda e, tt=tt: e.tensor_scalar(
                    xlnst[tt % 2][:, :], xt1[:, :], rstd2[:, tt:tt + 1],
                    negmr2[:, tt:tt + 1], ALU.mult, ALU.add), s_dv)
                b.wait("scalar", tcp)
                st1 = b.dma("scalar", xln_tm[2][tt * 128:(tt + 1) * 128, :],
                            xlnst[tt % 2][:, :], SC)
                st2 = b.dma("scalar", x2_hbm[tt * 128:(tt + 1) * 128, :], xt1[:, :],
                            SC)
                if DEBUG:
                    b.dma("scalar", dbg_x2[tt * 128:(tt + 1) * 128, :], xt1[:, :],
                          FIN)
                prev_sp2[tt % 2] = st1
                state["prev_xt1_free"] = st2

            cut(7)
            barrier()

            xb2 = [None] * ND
            for dt in range(ND):
                xb2[dt] = b.xbar("scalar", bigA[dt][:, :],
                                 xln_tm[2][:, dt * 128:(dt + 1) * 128], SC)
            for tt in range(NT):
                for dt in range(ND):
                    if tt == 0:
                        b.wait("tensor", xb2[dt])
                    t_router = b.run("tensor", lambda e, tt=tt, dt=dt: e.matmul(
                        P[tt // 8][:, (tt % 8) * 64:(tt % 8) * 64 + 64],
                        bigA[dt][:, tt * 128:(tt + 1) * 128],
                        rwT[:, dt * 64:(dt + 1) * 64],
                        start=(dt == 0), stop=(dt == ND - 1)), s_pe)

            state["prev_tail_act"] = None
            state["prev_tail_dve"] = None
            prev_ld_reader = None
            for tt in range(NT):
                b.wait("scalar", prev_ld_reader)
                ldx = b.dma("scalar", xt0[:, :],
                            x2_hbm[tt * 128:(tt + 1) * 128, :], SC)
                b.wait("vector", ldx)
                b.wait("scalar", ldx)
                prev_ld_reader = ffn_tail(tt, rstd2, negmr2, tt, True, None, None,
                                          0, None)

            dump_dbg()
            dump_dbg()
            fin_tok = FIN.val()
            b.wait("scalar", fin_tok)
            b.wait("sync", fin_tok)
            b.wait("gpsimd", fin_tok)

        except _Cut:
            pass

        with nc.Block() as block:

            @block.sync
            def _(eng):
                for fn in b.ops["sync"]:
                    fn(eng)

            @block.scalar
            def _(eng):
                for fn in b.ops["scalar"]:
                    fn(eng)

            @block.vector
            def _(eng):
                for fn in b.ops["vector"]:
                    fn(eng)

            @block.tensor
            def _(eng):
                for fn in b.ops["tensor"]:
                    fn(eng)

            @block.gpsimd
            def _(eng):
                for fn in b.ops["gpsimd"]:
                    fn(eng)

    return nc


_NC_CACHE = {}


def kernel(**inputs):
    x = np.ascontiguousarray(np.asarray(inputs["L0_attn_hidden_states"],
                                        dtype=np.float32))
    w_q = np.ascontiguousarray(np.asarray(inputs["L0_attn_w_q"], np.float32))
    w_k = np.ascontiguousarray(np.asarray(inputs["L0_attn_w_k"], np.float32))
    w_v = np.ascontiguousarray(np.asarray(inputs["L0_attn_w_v"], np.float32))
    w_o = np.ascontiguousarray(np.asarray(inputs["L0_attn_w_o"], np.float32))
    memories = np.ascontiguousarray(np.asarray(inputs["L0_attn_memories"],
                                               np.float32))
    mn = np.ascontiguousarray(
        np.asarray(inputs["L0_attn_memory_norms"], np.float32)
        .transpose(1, 0, 2).reshape(H * 4, 128))
    gate = np.asarray(inputs["L0_attn_gate"], np.float32).reshape(1, H)
    r_w = np.ascontiguousarray(np.asarray(inputs["L0_ffn_router_weight"],
                                          np.float32))
    r_b = np.asarray(inputs["L0_ffn_router_bias"], np.float32).reshape(1, 64)
    eb = np.ascontiguousarray(np.asarray(inputs["L0_ffn_expert_bias"],
                                         np.float32))

    identx = np.eye(128, dtype=ml_dtypes.bfloat16)
    maskTx = np.triu(np.ones((128, 128), dtype=np.float32)).astype(
        ml_dtypes.bfloat16)
    mgrpx = np.zeros((64, 4), dtype=ml_dtypes.bfloat16)
    for p in range(64):
        mgrpx[p, p % 4] = 1
    ones1x = np.ones((1, 128), dtype=ml_dtypes.bfloat16)

    if "nc" not in _NC_CACHE:
        _NC_CACHE["nc"] = build_program()
    nc = _NC_CACHE["nc"]

    in_maps = []
    for c in range(8):
        bi, hf = c // 2, c % 2
        sf = np.full((128, 1), 1.0 if hf == 0 else 0.0, np.float32)
        in_maps.append(dict(
            x0=np.ascontiguousarray(x[bi, hf * T:(hf + 1) * T, :]),
            w_q=w_q, w_k=w_k, w_v=w_v, w_o=w_o,
            mems=memories, mn=mn, gate=gate, r_w=r_w, r_b=r_b, eb=eb,
            identc=identx, maskTc=maskTx, mgrpc=mgrpx, ones1c=ones1x,
            sflagc=sf, rflagc=1.0 - sf,
        ))
    res = run_bass_kernel_spmd(nc, in_maps, core_ids=list(range(8)))
    out = np.empty((4, 4096, D), np.float32)
    for c in range(8):
        bi, hf = c // 2, c % 2
        out[bi, hf * T:(hf + 1) * T, :] = res.results[c]["y"]
    if DEBUG:
        kernel.dbg = res.results
    return out


if __name__ == "__main__":
    nc = build_program()
    print("build ok; instructions:", nc.n_instructions()
          if hasattr(nc, "n_instructions") else "?")


# version 07 (final)
# speedup vs baseline: 1.0469x; 1.0107x over previous
import numpy as np
import ml_dtypes
import concourse.bass as bass
import concourse.mybir as mybir
from concourse.bass_utils import run_bass_kernel_spmd
from contextlib import ExitStack

F32 = mybir.dt.float32
BF16 = mybir.dt.bfloat16
AF = mybir.ActivationFunctionType
ALU = mybir.AluOpType
AXX = mybir.AxisListType.X

D = 2048
T = 2048
H = 16
NT = 16
ND = 16
NCH = 16
SCALE = 128.0 ** -0.5
EPS_ATT = 1e-6
EPS_LN = 1e-5
DEBUG = False
import os
PH = int(os.environ.get("KPH", "9"))
KSUB = int(os.environ.get("KSUB", "99"))
KATT = int(os.environ.get("KATT", "16"))
KSB = int(os.environ.get("KSB", "99"))


def carve(nc, region, offset_bytes, name, shape, dtype):
    per_part = int(np.prod(shape[1:])) * mybir.dt.size(dtype)
    base = nc.lookup_mloc(region).addr
    assert offset_bytes % 32 == 0
    hnd = nc.alloc_sbuf_tensor_at(name, list(shape), dtype,
                                  offset=base + offset_bytes)
    return hnd, offset_bytes + ((per_part + 31) // 32) * 32


class Stream:

    def __init__(self, sems, step=16):
        self.sems = sems
        self.cnt = [0] * len(sems)
        self.i = 0
        self.step = step

    def next_slot(self):
        k = self.i % len(self.sems)
        self.i += 1
        prior = self.cnt[k]
        self.cnt[k] += self.step
        return self.sems[k], prior, self.cnt[k]

    def val(self):
        return tuple((s, c) for s, c in zip(self.sems, self.cnt) if c > 0)


def _flat(toks):
    out = []
    for t in toks:
        if t is None:
            continue
        if isinstance(t, tuple) and len(t) == 2 and not isinstance(t[0], tuple):
            out.append(t)
        else:
            out.extend(x for x in t if x is not None)
    return tuple(t for t in out if t[1] > 0)


class B:
    ENGS = ("sync", "scalar", "vector", "tensor", "gpsimd")

    def __init__(self):
        self.ops = {k: [] for k in self.ENGS}
        self.cnt = {}

    def _inc(self, sem, n):
        self.cnt[sem.name] = self.cnt.get(sem.name, 0) + n
        return (sem, self.cnt[sem.name])

    def val(self, sem):
        return (sem, self.cnt.get(sem.name, 0))

    def em(self, engine, fn):
        self.ops[engine].append(fn)

    def dma(self, engine, out_ap, in_ap, stream, transpose=False):
        sem, prior, newv = stream.next_slot()
        self.em(engine, lambda e, o=out_ap, i=in_ap, s=sem, p=prior,
                tr=transpose: (
            e.wait_ge(s, p) if p > 0 else None,
            (e.dma_start_transpose(out=o, in_=i) if tr
             else e.dma_start(out=o, in_=i)).then_inc(s, 16)))
        return (sem, newv)

    def xbar(self, engine, out_ap, in_ap, stream):
        return self.dma(engine, out_ap, in_ap, stream, transpose=True)

    def wait(self, engine, *toks):
        toks = _flat(toks)
        if toks:
            self.em(engine, lambda e, tk=toks: [e.wait_ge(s, v) for s, v in tk])

    def run(self, engine, fn, sem):
        prev = self.cnt.get(sem.name, 0)
        t = self._inc(sem, 1)
        if engine in ("vector", "scalar", "gpsimd"):
            self.em(engine, lambda e, f=fn, s=sem, p=prev: (
                e.wait_ge(s, p) if p > 0 else None, f(e).then_inc(s, 1)))
        else:
            self.em(engine, lambda e, f=fn, s=sem: f(e).then_inc(s, 1))
        return t

    def mark(self, engine, sem):
        t = self._inc(sem, 1)
        self.em(engine, lambda e, s=sem: e.sem_inc(s, 1))
        return t


def build_program():
    nc = bass.Bass("TRN2", target_bir_lowering=False)

    x0 = nc.dram_tensor("x0", [T, D], F32, kind="ExternalInput").ap()
    w_in = {n: nc.dram_tensor(f"w_{n}", [D, D], F32, kind="ExternalInput").ap()
            for n in "qkvo"}
    mems = nc.dram_tensor("mems", [4, H, 128, 128], F32, kind="ExternalInput").ap()
    mn_in = nc.dram_tensor("mn", [H * 4, 128], F32, kind="ExternalInput").ap()
    gate_in = nc.dram_tensor("gate", [1, H], F32, kind="ExternalInput").ap()
    rw_in = nc.dram_tensor("r_w", [64, D], F32, kind="ExternalInput").ap()
    rb_in = nc.dram_tensor("r_b", [1, 64], F32, kind="ExternalInput").ap()
    eb_in = nc.dram_tensor("eb", [64, D], F32, kind="ExternalInput").ap()
    identc = nc.dram_tensor("identc", [128, 128], BF16, kind="ExternalInput").ap()
    maskTc = nc.dram_tensor("maskTc", [128, 128], BF16, kind="ExternalInput").ap()
    mgrpc = nc.dram_tensor("mgrpc", [64, 4], BF16, kind="ExternalInput").ap()
    ones1c = nc.dram_tensor("ones1c", [1, 128], BF16, kind="ExternalInput").ap()
    sflagc = nc.dram_tensor("sflagc", [128, 1], F32, kind="ExternalInput").ap()
    rflagc = nc.dram_tensor("rflagc", [128, 1], F32, kind="ExternalInput").ap()

    y_out = nc.dram_tensor("y", [T, D], F32, kind="ExternalOutput").ap()
    if DEBUG:
        dbg_x1 = nc.dram_tensor("dbg_x1", [T, D], F32, kind="ExternalOutput").ap()
        dbg_x2 = nc.dram_tensor("dbg_x2", [T, D], F32, kind="ExternalOutput").ap()
        dbg_st = nc.dram_tensor("dbg_st", [128, 128], F32, kind="ExternalOutput").ap()

    wbf = {n: nc.dram_tensor(f"wbf_{n}", [D, D], BF16).ap() for n in "qkvo"}
    xln_tm = [nc.dram_tensor(f"xln{i}_tm", [T, D], BF16).ap() for i in range(3)]
    kp_hbm = nc.dram_tensor("kp_hbm", [H, T, 128], BF16).ap()
    v_hbm = nc.dram_tensor("v_hbm", [H, T, 128], BF16).ap()
    out_hbm = nc.dram_tensor("out_hbm", [H, T, 128], BF16).ap()
    x1_hbm = nc.dram_tensor("x1_hbm", [T, D], F32).ap()
    x2_hbm = nc.dram_tensor("x2_hbm", [T, D], F32).ap()
    cc_in = [nc.dram_tensor(f"cc_in{h}", [128, 136], F32) for h in range(H)]
    cc_out = [nc.dram_tensor(f"cc_out{h}", [128, 136], F32) for h in range(H)]

    with ExitStack() as ctx:
        E = ctx.enter_context
        bigA = [E(nc.sbuf_tensor(f"bigA{i}", [128, D], BF16)) for i in range(16)]
        bigB = [E(nc.sbuf_tensor(f"bigB{i}", [128, D], BF16)) for i in range(16)]
        R2 = E(nc.sbuf_tensor("R2", [128, 54 * 1024], mybir.dt.uint8))

        off = 0
        xt0, off = carve(nc, R2, off, "xt0", [128, D], F32)
        xt1, off = carve(nc, R2, off, "xt1", [128, D], F32)
        sqd, off = carve(nc, R2, off, "sqd", [128, D], F32)
        xlnst0, off = carve(nc, R2, off, "xlnst0", [128, D], BF16)
        xlnst1, off = carve(nc, R2, off, "xlnst1", [128, D], BF16)
        xlnst = [xlnst0, xlnst1]
        famF_end = off

        offP = 0
        qsb = []
        for i in range(2):
            t_, offP = carve(nc, R2, offP, f"qsb{i}", [128, 512], F32)
            qsb.append(t_)
        etm = []
        for i in range(2):
            t_, offP = carve(nc, R2, offP, f"etm{i}", [128, 1024], F32)
            etm.append(t_)
        kvst = []
        for i in range(2):
            t_, offP = carve(nc, R2, offP, f"kvst{i}", [128, 1024], BF16)
            kvst.append(t_)
        wT = []
        for i in range(16):
            t_, offP = carve(nc, R2, offP, f"wT{i}", [128, 1024], BF16)
            wT.append(t_)
        famP_end = offP

        offA = 0
        Lb = []
        for i in range(2):
            t_, offA = carve(nc, R2, offA, f"Lb{i}", [128, NCH * 129], BF16)
            Lb.append(t_)
        kpsd = []
        for i in range(2):
            t_, offA = carve(nc, R2, offA, f"kpsd{i}", [128, D], BF16)
            kpsd.append(t_)
        vaug = []
        for i in range(2):
            t_, offA = carve(nc, R2, offA, f"vaug{i}", [128, NCH * 132], BF16)
            vaug.append(t_)
        kpds = []
        for i in range(2):
            t_, offA = carve(nc, R2, offA, f"kpds{i}", [128, D], BF16)
            kpds.append(t_)
        outstS, steffS, scbS, ummS = [], [], [], []
        for i in range(2):
            t_, offA = carve(nc, R2, offA, f"outst{i}", [128, D], BF16)
            outstS.append(t_)
            t_, offA = carve(nc, R2, offA, f"steff{i}", [128, 129], BF16)
            steffS.append(t_)
            t_, offA = carve(nc, R2, offA, f"scb{i}", [128, 128], BF16)
            scbS.append(t_)
            t_, offA = carve(nc, R2, offA, f"umm{i}", [128, 4 * 128], F32)
            ummS.append(t_)
        famA_end = offA
        assert max(famP_end, famA_end) <= 54 * 1024, (famP_end, famA_end)

        ident = E(nc.sbuf_tensor("ident", [128, 128], BF16))
        maskT = E(nc.sbuf_tensor("maskT", [128, 128], BF16))
        ones1 = E(nc.sbuf_tensor("ones1", [1, 128], BF16))
        mgrp = E(nc.sbuf_tensor("mgrp", [64, 4], BF16))
        sflag = E(nc.sbuf_tensor("sflag", [128, 1], F32))
        rflag = E(nc.sbuf_tensor("rflag", [128, 1], F32))
        rwT = E(nc.sbuf_tensor("rwT", [128, 16 * 64], BF16))
        rw_bf = E(nc.sbuf_tensor("rw_bf", [64, D], BF16))
        eb_bf = E(nc.sbuf_tensor("eb_bf", [64, D], BF16))
        mn32 = E(nc.sbuf_tensor("mn32", [64, 128], F32))
        mn_bf = E(nc.sbuf_tensor("mn_bf", [64, 128], BF16))
        mnT = E(nc.sbuf_tensor("mnT", [128, 64], BF16))
        rbrow = E(nc.sbuf_tensor("rbrow", [1, 64], F32))
        rbrow_bf = E(nc.sbuf_tensor("rbrow_bf", [1, 64], BF16))
        rb_bc = E(nc.sbuf_tensor("rb_bc", [128, 64], F32))
        grow = E(nc.sbuf_tensor("grow", [1, 16], F32))
        grow_bf = E(nc.sbuf_tensor("grow_bf", [1, 16], BF16))
        g_bc = E(nc.sbuf_tensor("g_bc", [128, 16], F32))
        gm1_bc = E(nc.sbuf_tensor("gm1_bc", [128, 16], F32))
        erow = E(nc.sbuf_tensor("erow", [1, 16], F32))
        eU = E(nc.sbuf_tensor("eU", [1, 4], F32))
        rows_bf = E(nc.sbuf_tensor("rows_bf", [64, 1], BF16))
        relrow = E(nc.sbuf_tensor("relrow", [1, 64], F32))
        wgS = [E(nc.sbuf_tensor(f"wg{i}", [1, 12], F32)) for i in range(2)]
        wg_bfS = [E(nc.sbuf_tensor(f"wg_bf{i}", [1, 4], BF16)) for i in range(2)]
        wbcS = [E(nc.sbuf_tensor(f"wbc{i}", [128, 4], F32)) for i in range(2)]
        memh = [E(nc.sbuf_tensor(f"memh{i}", [128, 512], BF16)) for i in range(2)]
        qsum4 = E(nc.sbuf_tensor("qsum4", [128, 64], F32))
        qsum_bf = E(nc.sbuf_tensor("qsum_bf", [128, 16], BF16))
        qsum_f = E(nc.sbuf_tensor("qsum_f", [128, 16], F32))
        ccsb = E(nc.sbuf_tensor("ccsb", [128, 136], F32))
        ccrvS = [E(nc.sbuf_tensor(f"ccrv{i}", [128, 136], F32)) for i in range(2)]
        mcol = E(nc.sbuf_tensor("mcol", [128, 16], F32))
        rstd0 = E(nc.sbuf_tensor("rstd0", [128, 16], F32))
        negmr0 = E(nc.sbuf_tensor("negmr0", [128, 16], F32))
        rstd1 = E(nc.sbuf_tensor("rstd1", [128, 2], F32))
        negmr1 = E(nc.sbuf_tensor("negmr1", [128, 2], F32))
        rstd2 = E(nc.sbuf_tensor("rstd2", [128, 16], F32))
        negmr2 = E(nc.sbuf_tensor("negmr2", [128, 16], F32))
        sumc = E(nc.sbuf_tensor("sumc", [128, 48], F32))
        sqc = E(nc.sbuf_tensor("sqc", [128, 48], F32))
        colt = E(nc.sbuf_tensor("colt", [128, 8], F32))
        lncol = E(nc.sbuf_tensor("lncol", [128, 96], F32))
        swtc = E(nc.sbuf_tensor("swtc", [128, 16], F32))
        top8 = E(nc.sbuf_tensor("top8", [128, 8], F32))
        lgsb = E(nc.sbuf_tensor("lgsb", [128, 64], F32))
        esb = E(nc.sbuf_tensor("esb", [128, 64], F32))
        e8sb = E(nc.sbuf_tensor("e8sb", [128, 8], F32))
        mskb = E(nc.sbuf_tensor("mskb", [128, 64], F32))
        msbf = E(nc.sbuf_tensor("msbf", [128, 64], BF16))
        msT = E(nc.sbuf_tensor("msT", [64, 16 * 128], BF16))
        dcolS = [E(nc.sbuf_tensor(f"dcol{i}", [128, 4], F32)) for i in range(2)]
        ncolS = [E(nc.sbuf_tensor(f"ncol{i}", [128, 8], F32)) for i in range(2)]

        P = [E(nc.psum_tensor(f"P{i}", [128, 512], F32)) for i in range(7)]
        PT = E(nc.psum_tensor("PT", [128, 1024], BF16))

        SY = Stream([E(nc.semaphore(f"sy{i}")) for i in range(6)])
        SC = Stream([E(nc.semaphore(f"sc{i}")) for i in range(6)])
        GW = Stream([E(nc.semaphore(f"gw{i}")) for i in range(4)])
        GP = Stream([E(nc.semaphore(f"gp{i}")) for i in range(4)])
        CC = Stream([E(nc.semaphore(f"cc{i}")) for i in range(int(os.environ.get("KCC", "4")))], step=1)
        FIN = Stream([E(nc.semaphore(f"fin{i}")) for i in range(3)])
        s_pe = E(nc.semaphore("s_pe"))
        s_dv = E(nc.semaphore("s_dv"))
        s_ac = E(nc.semaphore("s_ac"))

        b = B()

        def barrier():
            waits = (b.val(s_pe), b.val(s_dv), b.val(s_ac), SY.val(),
                     SC.val(), GW.val(), GP.val())
            for eng in B.ENGS:
                b.wait(eng, *waits)

        def ln_stats(xt, rstd_a, negmr_a, ci, scol, dep):
            b.wait("vector", dep)
            b.run("vector", lambda e: e.reduce_sum(
                sumc[:, scol:scol + 1], xt[:, :], axis=AXX), s_dv)
            b.wait("scalar", dep)
            t2 = b.run("scalar", lambda e: e.activation(
                sqd[:, :], xt[:, :], AF.Square, accum_out=sqc[:, scol:scol + 1]),
                s_ac)
            b.wait("vector", t2)
            b.run("vector", lambda e: e.tensor_scalar(
                mcol[:, ci:ci + 1], sumc[:, scol:scol + 1], 1.0 / D, None,
                ALU.mult), s_dv)
            b.run("vector", lambda e: e.tensor_scalar(
                colt[:, 0:1], sqc[:, scol:scol + 1], 1.0 / D, None, ALU.mult),
                s_dv)
            b.run("vector", lambda e: e.tensor_scalar(
                colt[:, 1:2], mcol[:, ci:ci + 1], mcol[:, ci:ci + 1], None,
                ALU.mult), s_dv)
            t3 = b.run("vector", lambda e: e.tensor_scalar(
                colt[:, 0:1], colt[:, 0:1], colt[:, 1:2], EPS_LN,
                ALU.subtract, ALU.add), s_dv)
            b.wait("scalar", t3)
            b.run("scalar", lambda e: e.activation(colt[:, 2:3], colt[:, 0:1],
                                                   AF.Ln), s_ac)
            t5 = b.run("scalar", lambda e: e.activation(
                rstd_a[:, ci:ci + 1], colt[:, 2:3], AF.Exp, scale=-0.5), s_ac)
            b.wait("vector", t5)
            return b.run("vector", lambda e: e.tensor_scalar(
                negmr_a[:, ci:ci + 1], mcol[:, ci:ci + 1], rstd_a[:, ci:ci + 1],
                -1.0, ALU.mult, ALU.mult), s_dv)

        for dst, src in ((ident, identc), (maskT, maskTc), (ones1, ones1c),
                         (mgrp, mgrpc), (sflag, sflagc), (rflag, rflagc),
                         (mn32, mn_in)):
            b.dma("sync", dst[:, :], src[:, :], SY)
        b.dma("sync", grow[0:1, :], gate_in[:, :], SY)
        b.dma("sync", rbrow[0:1, :], rb_in[:, :], SY)
        tconsts = SY.val()
        t_rwbf = b.dma("gpsimd", rw_bf[:, :], rw_in[:, :], GW)
        t_ebbf = b.dma("gpsimd", eb_bf[:, :], eb_in[:, :], GW)
        t_mnbf = b.dma("gpsimd", mn_bf[:, :], mn_in[:, :], GW)

        b.wait("tensor", t_rwbf, tconsts)
        for dt in range(ND):
            tp = b.run("tensor", lambda e, dt=dt: e.transpose(
                PT[:, 0:64], rw_bf[0:64, dt * 128:(dt + 1) * 128],
                ident[0:64, 0:64]),
                s_pe)
            b.wait("vector", tp)
            tv = b.run("vector", lambda e, dt=dt: e.tensor_copy(
                rwT[:, dt * 64:(dt + 1) * 64], PT[:, 0:64]), s_dv)
            b.wait("tensor", tv)
        b.wait("tensor", t_mnbf)
        tp = b.run("tensor", lambda e: e.transpose(PT[:, 64:128], mn_bf[:, :],
                                                   ident[0:64, 0:64]), s_pe)
        b.wait("vector", tp)
        t_mnT = b.run("vector", lambda e: e.tensor_copy(mnT[:, :], PT[:, 64:128]),
                      s_dv)

        b.wait("vector", tconsts)
        t1 = b.run("vector", lambda e: e.tensor_copy(rbrow_bf[:, :],
                                                     rbrow[0:1, :]), s_dv)
        b.wait("tensor", t1)
        t2 = b.run("tensor", lambda e: e.matmul(P[6][:, 0:64], ones1[:, :],
                                                rbrow_bf[:, :], start=True,
                                                stop=True), s_pe)
        b.wait("vector", t2)
        b.run("vector", lambda e: e.tensor_copy(rb_bc[:, :], P[6][:, 0:64]), s_dv)

        b.wait("scalar", tconsts)
        t1 = b.run("scalar", lambda e: e.activation(erow[0:1, 0:16], grow[0:1, :],
                                                    AF.Exp, scale=-1.0), s_ac)
        b.wait("vector", t1)
        b.run("vector", lambda e: e.tensor_scalar(erow[0:1, 0:16],
                                                  erow[0:1, 0:16], 1.0, None,
                                                  ALU.add), s_dv)
        b.run("vector", lambda e: e.reciprocal(grow[0:1, :], erow[0:1, 0:16]),
              s_dv)
        t1 = b.run("vector", lambda e: e.tensor_copy(grow_bf[:, :], grow[0:1, :]),
                   s_dv)
        b.wait("tensor", t1)
        t2 = b.run("tensor", lambda e: e.matmul(P[6][:, 64:80], ones1[:, :],
                                                grow_bf[:, :], start=True,
                                                stop=True), s_pe)
        b.wait("vector", t2)
        b.run("vector", lambda e: e.tensor_copy(g_bc[:, :], P[6][:, 64:80]), s_dv)
        b.run("vector", lambda e: e.tensor_scalar(gm1_bc[:, :], g_bc[:, :], -1.0,
                                                  1.0, ALU.mult, ALU.add), s_dv)

        t1 = b.run("vector", lambda e: e.reduce_sum(colt[0:64, 0:1], mn32[:, :],
                                                    axis=AXX), s_dv)
        b.wait("vector", t1)
        t2 = b.run("vector", lambda e: e.tensor_copy(rows_bf[:, :],
                                                     colt[0:64, 0:1]), s_dv)
        b.wait("tensor", t2)
        t3 = b.run("tensor", lambda e: e.matmul(P[6][0:1, 128:132], rows_bf[:, :],
                                                mgrp[:, :], start=True,
                                                stop=True), s_pe)
        b.wait("vector", t3)
        t4 = b.run("vector", lambda e: e.tensor_scalar(
            erow[0:1, 0:4], P[6][0:1, 128:132], 1.0, -EPS_ATT, ALU.mult,
            ALU.add), s_dv)
        b.wait("scalar", t4)
        t5 = b.run("scalar", lambda e: e.sign(erow[0:1, 4:8], erow[0:1, 0:4]),
                   s_ac)
        b.wait("vector", t5)
        b.run("vector", lambda e: e.tensor_scalar(erow[0:1, 4:8], erow[0:1, 4:8],
                                                  -1.0, 0.0, ALU.mult, ALU.max),
              s_dv)
        b.run("vector", lambda e: e.tensor_scalar(eU[0:1, :], erow[0:1, 4:8],
                                                  -1.0, 1.0, ALU.mult, ALU.add),
              s_dv)
        for z in (qsum4, sqc, sumc, ccsb):
            b.run("vector", lambda e, z=z: e.memset(z[:, :], 0.0), s_dv)

        class _Cut(Exception):
            pass

        def cut(k):
            if PH <= k:
                dump_dbg()
                junk_finish()
                raise _Cut()

        def dump_dbg():
            if not DEBUG:
                return
            b.wait("gpsimd", b.val(s_dv), b.val(s_ac), b.val(s_pe),
                   SY.val(), SC.val())
            b.dma("gpsimd", dbg_st[:, 0:64], lgsb[:, :], GP)
            b.dma("gpsimd", dbg_st[:, 64:72], top8[:, :], GP)
            b.dma("gpsimd", dbg_st[:, 72:88], swtc[:, :], GP)
            b.dma("gpsimd", dbg_st[:, 88:104], esb[:, 0:16], GP)
            b.dma("gpsimd", dbg_st[:, 104:120], rstd0[:, :], GP)
            if PH == 3:
                b.dma("gpsimd", dbg_x2[0:64, :], msT[:, :], GP)
                b.dma("gpsimd", dbg_x2[128:256, :], bigA[0][:, :], GP)
                b.dma("gpsimd", dbg_x2[256:384, :], xln_tm[0][1792:1920, :], GP)
            if PH == 7:
                b.dma("gpsimd", dbg_st[0:1, 0:64], relrow[0:1, :], GP)
                b.dma("gpsimd", dbg_st[0:128, 64:80], g_bc[:, :], GP)
                b.dma("gpsimd", dbg_st[0:128, 80:84], wbc[:, :], GP)
                b.dma("gpsimd", dbg_st[0:128, 84:92], ncol[:, :], GP)
            if PH in (5, 7):
                src_d = out_hbm if PH == 7 else kp_hbm
                for hh_ in range(16):
                    b.dma("gpsimd",
                          dbg_x1[hh_ * 128:(hh_ + 1) * 128, :].rearrange(
                              "p (n e) -> p n e", e=128),
                          v_hbm[hh_][:, :].rearrange("(n p) e -> p n e",
                                                     p=128), GP)
                for hh_ in range(16):
                    b.dma("gpsimd",
                          dbg_x2[hh_ * 128:(hh_ + 1) * 128, :].rearrange(
                              "p (n e) -> p n e", e=128),
                          src_d[hh_][:, :].rearrange("(n p) e -> p n e",
                                                     p=128), GP)
            b.wait("gpsimd", GP.val())

        def junk_finish():
            toks = []
            for tt in range(NT):
                toks.append(b.dma("sync", y_out[tt * 128:(tt + 1) * 128, :],
                                  x0[tt * 128:(tt + 1) * 128, :], FIN))
            b.wait("sync", toks[-1])
            b.wait("scalar", toks[-1])
            b.wait("gpsimd", toks[-1])
            b.wait("vector", toks[-1])
            b.wait("tensor", toks[-1])

        try:
            xtl = [xt0, xt1]
            ld = [None] * NT
            ld[0] = b.dma("sync", xtl[0][:, :], x0[0:128, :], SY)
            ld[1] = b.dma("sync", xtl[1][:, :], x0[128:256, :], SY)
            b.wait("scalar", b.val(s_dv))
            prev_st = [None, None]
            gate_tok = None
            for tt in range(NT):
                xt = xtl[tt % 2]
                t6 = ln_stats(xt, rstd0, negmr0, tt, tt, ld[tt])
                b.wait("vector", prev_st[tt % 2], t6)
                tcp = b.run("vector", lambda e, tt=tt, xt=xt: e.tensor_scalar(
                    xlnst[tt % 2][:, :], xt[:, :], rstd0[:, tt:tt + 1],
                    negmr0[:, tt:tt + 1], ALU.mult, ALU.add), s_dv)
                b.wait("scalar", tcp)
                st = b.dma("scalar", xln_tm[0][tt * 128:(tt + 1) * 128, :],
                           xlnst[tt % 2][:, :], SC)
                if tt == 0:
                    gate_tok = st
                prev_st[tt % 2] = st
                if tt + 2 < NT:
                    b.wait("sync", tcp)
                    ld[tt + 2] = b.dma("sync", xt[:, :],
                                       x0[(tt + 2) * 128:(tt + 3) * 128, :], SY)
            b.wait("gpsimd", gate_tok)
            wtok = {n: b.dma("gpsimd", wbf[n][:, :], w_in[n][:, :], GW)
                    for n in "qkvo"}
            t_xln0_done = SC.val()
            cut(1)

            xb = [None] * ND
            for dt in range(ND):
                b.wait("scalar", t_xln0_done)
                xb[dt] = b.xbar("scalar", bigA[dt][:, :],
                                xln_tm[0][:, dt * 128:(dt + 1) * 128], SC)
            b.wait("tensor", t_ebbf)
            for tt in range(NT):
                for dt in range(ND):
                    if tt == 0:
                        b.wait("tensor", xb[dt])
                    t_router = b.run("tensor", lambda e, tt=tt, dt=dt: e.matmul(
                        P[tt // 8][:, (tt % 8) * 64:(tt % 8) * 64 + 64],
                        bigA[dt][:, tt * 128:(tt + 1) * 128],
                        rwT[:, dt * 64:(dt + 1) * 64],
                        start=(dt == 0), stop=(dt == ND - 1)), s_pe)
            cut(2)

            state = {"prev_tail_act": None, "prev_tail_dve": None,
                     "prev_xt1_free": None, "prev_st": [None, None],
                     "prev_adds": None}

            def ffn_tail(tt, rstd_a, negmr_a, ci, phase2, xln_dst, x_dst, scol,
                         dbg_dst):
                b.wait("vector", state["prev_tail_act"], t_router,
                       b.val(s_pe), b.val(s_ac))
                b.wait("scalar", state["prev_tail_dve"], b.val(s_dv),
                       b.val(s_pe))
                b.wait("tensor", b.val(s_dv), b.val(s_ac))
                if KSUB == 10:
                    b.run("vector", lambda e: e.tensor_tensor(
                        lgsb[:, :], rb_bc[:, :], rb_bc[:, :], ALU.add), s_dv)
                    return None
                b.run("vector", lambda e, tt=tt: e.tensor_tensor(
                    lgsb[:, :], P[tt // 8][:, (tt % 8) * 64:(tt % 8) * 64 + 64],
                    rb_bc[:, :], ALU.add), s_dv)
                if KSUB == 11:
                    return None
                b.run("vector", lambda e: e.max(out=top8[:, :], in_=lgsb[:, :]), s_dv)
                t1 = b.run("vector", lambda e: e.tensor_scalar(
                    colt[:, 3:4], top8[:, 0:1], -1.0, None, ALU.mult), s_dv)
                if KSUB == 12:
                    return None
                b.wait("scalar", t1)
                t2 = b.run("scalar", lambda e: e.activation(
                    esb[:, :], lgsb[:, :], AF.Exp, bias=colt[:, 3:4]), s_ac)
                t3 = b.run("scalar", lambda e: e.activation(
                    e8sb[:, :], top8[:, :], AF.Exp, bias=colt[:, 3:4]), s_ac)
                b.wait("vector", t2, t3)
                if KSUB == 13:
                    return None
                b.run("vector", lambda e: e.reduce_sum(colt[:, 4:5], esb[:, :],
                                                       axis=AXX), s_dv)
                b.run("vector", lambda e: e.reduce_sum(colt[:, 5:6], e8sb[:, :],
                                                       axis=AXX), s_dv)
                b.run("vector", lambda e: e.reciprocal(colt[:, 6:7], colt[:, 4:5]),
                      s_dv)
                b.run("vector", lambda e: e.tensor_scalar(
                    colt[:, 6:7], colt[:, 6:7], 0.5, None, ALU.mult), s_dv)
                b.run("vector", lambda e, tt=tt: e.tensor_scalar(
                    swtc[:, tt:tt + 1], colt[:, 5:6], colt[:, 6:7], None, ALU.mult),
                    s_dv)
                if KSUB == 14:
                    return None
                b.run("vector", lambda e: e.tensor_scalar(
                    mskb[:, :], lgsb[:, :], top8[:, 7:8], None, ALU.is_ge), s_dv)
                t4 = b.run("vector", lambda e: e.scalar_tensor_tensor(
                    msbf[:, :], esb[:, :], colt[:, 6:7], mskb[:, :], ALU.mult,
                    ALU.mult), s_dv)
                state["prev_tail_act"] = t3
                state["prev_tail_dve"] = t4
                if KSUB < 2:
                    return None
                b.wait("tensor", t4)
                t5 = b.run("tensor", lambda e: e.transpose(
                    PT[0:64, 128:256], msbf[:, :], ident[:, :]), s_pe)
                b.wait("vector", t5)
                t6 = b.run("vector", lambda e, tt=tt: e.tensor_copy(
                    msT[:, tt * 128:(tt + 1) * 128], PT[0:64, 128:256]), s_dv)
                b.wait("tensor", t6, state.get("prev_adds"))
                if KSUB < 3:
                    return None
                mm = []
                for ec in range(4):
                    mm.append(b.run("tensor", lambda e, tt=tt, ec=ec: e.matmul(
                        P[2 + ec][:, :], msT[:, tt * 128:(tt + 1) * 128],
                        eb_bf[:, ec * 512:(ec + 1) * 512], start=True, stop=True),
                        s_pe))
                if KSUB < 4:
                    return None
                b.run("vector", lambda e, tt=tt: e.tensor_scalar(
                    colt[:, 7:8], swtc[:, tt:tt + 1], rstd_a[:, ci:ci + 1], None,
                    ALU.mult), s_dv)
                t7 = b.run("vector", lambda e: e.tensor_scalar(
                    colt[:, 7:8], colt[:, 7:8], 1.0, None, ALU.add), s_dv)
                t8 = b.run("vector", lambda e, tt=tt: e.tensor_scalar(
                    colt[:, 2:3], swtc[:, tt:tt + 1], negmr_a[:, ci:ci + 1], None,
                    ALU.mult), s_dv)
                b.wait("vector", t7, t8, state["prev_xt1_free"],
                       state.get("prev_xt1_b"))
                t9 = b.run("vector", lambda e: e.tensor_scalar(
                    xt1[:, :], xt0[:, :], colt[:, 7:8], colt[:, 2:3],
                    ALU.mult, ALU.add), s_dv)
                b.wait("vector", *mm)
                adds = []
                for ec in range(4):
                    adds.append(b.run("vector", lambda e, ec=ec: e.tensor_tensor(
                        xt1[:, ec * 512:(ec + 1) * 512],
                        xt1[:, ec * 512:(ec + 1) * 512], P[2 + ec][:, :], ALU.add),
                        s_dv))
                state["prev_tail_act"] = t9
                state["prev_tail_dve"] = adds[-1]
                state["prev_adds"] = adds[-1]
                if KSUB < 5:
                    state["prev_tail_dve"] = adds[-1]
                    return None
                if phase2:
                    b.wait("scalar", *adds)
                    st = b.dma("scalar", y_out[tt * 128:(tt + 1) * 128, :],
                               xt1[:, :], FIN)
                    state["prev_xt1_free"] = st
                    return t9
                tln = ln_stats(xt1, rstd1, negmr1, tt % 2, scol, adds[-1])
                b.wait("vector", tln, state["prev_st"][tt % 2])
                tcp = b.run("vector", lambda e, tt=tt: e.tensor_scalar(
                    xlnst[tt % 2][:, :], xt1[:, :],
                    rstd1[:, (tt % 2):(tt % 2) + 1],
                    negmr1[:, (tt % 2):(tt % 2) + 1], ALU.mult, ALU.add), s_dv)
                b.wait("scalar", tcp)
                st1 = b.dma("scalar", xln_dst[tt * 128:(tt + 1) * 128, :],
                            xlnst[tt % 2][:, :], SC)
                st2 = b.dma("scalar", x_dst[tt * 128:(tt + 1) * 128, :], xt1[:, :],
                            SC)
                if dbg_dst is not None:
                    st2 = b.dma("scalar", dbg_dst[tt * 128:(tt + 1) * 128, :],
                                xt1[:, :], FIN)
                    b.wait("gpsimd", tcp)
                    st2b = b.dma("gpsimd", dbg_x2[tt * 128:(tt + 1) * 128, :],
                                 xt1[:, :], GP)
                    state["prev_xt1_b"] = st2b
                state["prev_st"][tt % 2] = st1
                state["prev_xt1_free"] = st2
                state["prev_tail_act"] = tcp
                return tcp

            b.wait("sync", b.val(s_dv), b.val(s_ac))
            prev_ld_reader = None
            for tt in range(NT):
                b.wait("sync", prev_ld_reader)
                ldx = b.dma("sync", xt0[:, :], x0[tt * 128:(tt + 1) * 128, :], SY)
                b.wait("vector", ldx)
                b.wait("scalar", ldx)
                prev_ld_reader = ffn_tail(
                    tt, rstd0, negmr0, tt, False, xln_tm[1], x1_hbm, 16 + tt,
                    dbg_x1 if DEBUG else None)

            cut(3)
            barrier()

            xb1 = [None] * ND
            for dt in range(ND):
                xb1[dt] = b.xbar("scalar", bigA[dt][:, :],
                                 xln_tm[1][:, dt * 128:(dt + 1) * 128], SC)

            def stream_wT(name, half):
                toks = []
                dep = b.val(s_pe)
                for dt in range(ND):
                    b.wait("scalar", wtok[name], dep)
                    toks.append(b.xbar("scalar", wT[dt][:, :],
                                       wbf[name][half * 1024:(half + 1) * 1024,
                                                 dt * 128:(dt + 1) * 128], SC))
                return toks

            prev_stt = [None, None]
            for half in range(2):
                wts = stream_wT("q", half)
                prev_evac = {k_: b.val(s_dv) for k_ in range(4)}
                for ei in range(8):
                    et = half * 8 + ei
                    for pair in range(2):
                        bk = (P[0], P[1]) if pair == 0 else (P[2], P[3])
                        b.wait("tensor", prev_evac.get(pair * 2),
                               prev_evac.get(pair * 2 + 1))
                        for dt in range(ND):
                            if ei == 0 and pair == 0:
                                b.wait("tensor", wts[dt], xb1[dt])
                            for tci in range(2):
                                tc = pair * 2 + tci
                                tmm_t = b.run("tensor", lambda e, et=et, dt=dt,
                                              tc=tc, bb=bk[tci]: e.matmul(
                                          bb[:, :],
                                          wT[dt][:, (et % 8) * 128:
                                                 (et % 8) * 128 + 128],
                                          bigA[dt][:, tc * 512:(tc + 1) * 512],
                                          start=(dt == 0), stop=(dt == ND - 1)),
                                      s_pe)
                        for tci in range(2):
                            tc = pair * 2 + tci
                            bb = bk[tci]
                            p = tc % 2
                            b.wait("scalar", tmm_t, prev_stt[p])
                            ta = b.run("scalar", lambda e, bb=bb, p=p, et=et,
                                       tc=tc:
                                  e.activation(qsb[p][:, :], bb[:, :], AF.Copy,
                                               accum_out=qsum4[:, et * 4 + tc:
                                                               et * 4 + tc + 1]),
                                  s_ac)
                            b.wait("vector", ta)
                            tb = b.run("vector", lambda e, p=p: e.tensor_scalar(
                                etm[p][:, 0:512], qsb[p][:, :], 0.0, None, ALU.min),
                                s_dv)
                            b.wait("scalar", tb)
                            tc2 = b.run("scalar", lambda e, p=p: e.activation(
                                etm[p][:, 512:1024], etm[p][:, 0:512], AF.Exp),
                                s_ac)
                            b.wait("vector", tc2)
                            td = b.run("vector", lambda e, p=p, et=et, tc=tc:
                                       e.scalar_tensor_tensor(
                                           bigB[et][:, tc * 512:(tc + 1) * 512],
                                           qsb[p][:, :], 0.0, etm[p][:, 512:1024],
                                           ALU.max, ALU.add), s_dv)
                            prev_evac[tc] = td
                            prev_stt[p] = td

            t1 = b.run("vector", lambda e: e.reduce_sum(
                qsum_f[:, :], qsum4[:, :].rearrange("p (h f) -> p h f", f=4),
                axis=AXX), s_dv)
            b.wait("vector", t1)
            t_qs = b.run("vector", lambda e: e.tensor_copy(qsum_bf[:, :],
                                                           qsum_f[:, :]), s_dv)
            b.wait("tensor", t_qs, t_mnT)
            for h in range(H):
                t_rel = b.run("tensor", lambda e, h=h: e.matmul(
                    P[6][0:1, 256 + h * 4:256 + h * 4 + 4], qsum_bf[:, h:h + 1],
                    mnT[:, h * 4:h * 4 + 4], start=True, stop=True), s_pe)
            b.wait("vector", t_rel)
            t_relv = b.run("vector", lambda e: e.tensor_scalar(
                relrow[0:1, :], P[6][0:1, 256:320], SCALE / 4096.0, None, ALU.mult),
                s_dv)

            cut(4)
            def proj_b(name, dest_hbm, do_epo, init_sp=None):
                prev_sp_slot = init_sp if init_sp is not None else [None, None]
                for half in range(2):
                    wts = stream_wT(name, half)
                    prev_ev = [b.val(s_dv), b.val(s_dv)]
                    for tt in range(NT):
                        ps_ = tt % 2
                        bk = (P[0], P[1]) if ps_ == 0 else (P[2], P[3])
                        b.wait("tensor", prev_ev[ps_])
                        for dt in range(ND):
                            if tt == 0:
                                b.wait("tensor", wts[dt], xb1[dt])
                            for eci in range(2):
                                tmm_t = b.run("tensor", lambda e, tt=tt, dt=dt,
                                              eci=eci, bb=bk[eci]: e.matmul(
                                          bb[:, :],
                                          bigA[dt][:, tt * 128:(tt + 1) * 128],
                                          wT[dt][:, eci * 512:(eci + 1) * 512],
                                          start=(dt == 0), stop=(dt == ND - 1)),
                                      s_pe)
                        last = None
                        for eci in range(2):
                            bb = bk[eci]
                            p = tt % 2
                            if do_epo:
                                b.wait("vector", tmm_t, prev_sp_slot[p])
                                tb = b.run("vector", lambda e, bb=bb, p=p:
                                           e.tensor_scalar(etm[p][:, 0:512],
                                                           bb[:, :], 0.0, None,
                                                           ALU.min), s_dv)
                                b.wait("scalar", tb)
                                tc2 = b.run("scalar", lambda e, p=p: e.activation(
                                    etm[p][:, 512:1024], etm[p][:, 0:512], AF.Exp),
                                    s_ac)
                                b.wait("vector", tc2)
                                last = b.run(
                                    "vector", lambda e, bb=bb, p=p, eci=eci:
                                    e.scalar_tensor_tensor(
                                        kvst[p][:, eci * 512:(eci + 1) * 512],
                                        bb[:, :], 0.0, etm[p][:, 512:1024],
                                        ALU.max, ALU.add), s_dv)
                            else:
                                b.wait("vector", tmm_t, prev_sp_slot[p])
                                last = b.run("vector", lambda e, bb=bb, p=p,
                                             eci=eci: e.tensor_copy(
                                                 kvst[p][:, eci * 512:
                                                         (eci + 1) * 512],
                                                 bb[:, :]), s_dv)
                            prev_ev[ps_] = last
                        b.wait("sync", last)
                        prev_sp_slot[tt % 2] = b.dma(
                            "sync",
                            dest_hbm[half * 8:(half + 1) * 8,
                                     tt * 128:(tt + 1) * 128, :].rearrange(
                                         "h p e -> p h e"),
                            kvst[tt % 2][:, :].rearrange("p (h e) -> p h e", e=128),
                            SY)
                return prev_sp_slot

            sp_k = proj_b("k", kp_hbm, True)
            proj_b("v", v_hbm, False, sp_k)

            cut(5)
            barrier()

            for i in range(2):
                b.run("vector", lambda e, i=i: e.memset(
                    vaug[i][:, :].rearrange("p (n c) -> p n c", c=132)[:, :, 128:129],
                    1.0), s_dv)
            t_vone = b.val(s_dv)

            attn_state = {"prev_Lcopy": None}

            def emit_A(h, dep_tensor, dep_vec):
                sl = h % 2
                b.wait("sync", dep_tensor)
                tk = b.dma("sync",
                           kpsd[sl][:, :].rearrange("p (n e) -> p n e", e=128),
                           kp_hbm[h][:, :].rearrange("(n p) e -> p n e", p=128),
                           SY)
                tv = b.dma("sync",
                           vaug[sl][:, :].rearrange("p (n c) -> p n c",
                                                    c=132)[:, :, 0:128],
                           v_hbm[h][:, :].rearrange("(n p) e -> p n e", p=128),
                           SY)
                b.wait("tensor", tk, tv, t_vone)
                b.wait("scalar", dep_vec)
                tcopy = attn_state["prev_Lcopy"]
                for j in range(NCH):
                    if tcopy is not None:
                        b.wait("tensor", tcopy)
                    tj = b.run("tensor", lambda e, sl=sl, j=j: e.matmul(
                        P[5][:, 0:129], kpsd[sl][:, j * 128:(j + 1) * 128],
                        vaug[sl][:, j * 132:j * 132 + 129],
                        start=(j == 0), stop=(j == NCH - 1)), s_pe)
                    b.wait("scalar", tj)
                    tcopy = b.run("scalar", lambda e, sl=sl, j=j: e.activation(
                        Lb[sl][:, j * 129:(j + 1) * 129], P[5][:, 0:129],
                        AF.Copy), s_ac)
                attn_state["prev_Lcopy"] = tcopy
                b.wait("vector", GP.val(), tcopy)
                t1 = b.run("vector", lambda e, sl=sl: e.tensor_scalar(
                    ccsb[:, 0:129], Lb[sl][:, 15 * 129:16 * 129], sflag[:, 0:1],
                    None, ALU.mult), s_dv)
                b.wait("vector", t_relv)
                t2 = b.run("vector", lambda e, h=h: e.tensor_copy(
                    ccsb[0:1, 129:133], relrow[0:1, h * 4:h * 4 + 4]), s_dv)
                b.wait("gpsimd", t1, t2)
                b.dma("gpsimd", cc_in[h].ap(), ccsb[:, :], GP)
                sem, prior, newv = CC.next_slot()
                b.em("gpsimd", lambda e, h=h, s=sem, p=prior: (
                    e.wait_ge(s, p) if p > 0 else None,
                    e.collective_compute(
                        "AllReduce", ALU.add,
                        replica_groups=[[0, 1], [2, 3], [4, 5], [6, 7]],
                        ins=[cc_in[h].ap().opt()], outs=[cc_out[h].ap().opt()]
                    ).then_inc(s, 1)))
                return (sem, newv)

            def emit_B_hdr(h, st, tcc, dep_dve, dep_sp):
                sl = h % 2
                b.wait("vector", b.val(s_ac), b.val(s_pe), SC.val(), dep_sp)
                b.wait("scalar", b.val(s_dv))
                b.wait("gpsimd", b.val(s_pe))
                tmh = b.dma("gpsimd", memh[sl][:, :].rearrange(
                    "p (m e) -> p m e", e=128),
                    mems[:, h].rearrange("m p e -> p m e"), GW)
                b.wait("sync", b.val(s_pe))
                tds = b.xbar("sync", kpds[sl][:, :], kp_hbm[h][:, :], SY)
                b.wait("gpsimd", tcc, dep_dve)
                ccrv, wg, wg_bf, wbc = ccrvS[st], wgS[st], wg_bfS[st], wbcS[st]
                wcol = 320 + st * 8
                trv = b.dma("gpsimd", ccrv[:, :], cc_out[h].ap(), GP)
                b.wait("vector", trv)
                tse = b.run("vector", lambda e, st=st: e.tensor_scalar(
                    steffS[st][:, :], ccrvS[st][:, 0:129], rflag[:, 0:1], None,
                    ALU.mult), s_dv)
                b.run("vector", lambda e: e.tensor_tensor(
                    wg[0:1, 0:4], ccrv[0:1, 129:133], eU[0:1, :], ALU.mult), s_dv)
                b.run("vector", lambda e: e.scalar_tensor_tensor(
                    wg[0:1, 0:4], erow[0:1, 4:8], -1e30, wg[0:1, 0:4], ALU.mult,
                    ALU.add), s_dv)
                b.run("vector", lambda e: e.reduce_max(wg[0:1, 4:5], wg[0:1, 0:4],
                                                       axis=AXX), s_dv)
                t1 = b.run("vector", lambda e: e.tensor_scalar(
                    wg[0:1, 5:6], wg[0:1, 4:5], -1.0, None, ALU.mult), s_dv)
                b.wait("scalar", t1)
                t2 = b.run("scalar", lambda e: e.activation(
                    wg[0:1, 0:4], wg[0:1, 0:4], AF.Exp, bias=wg[0:1, 5:6]), s_ac)
                b.wait("vector", t2)
                b.run("vector", lambda e: e.reduce_sum(wg[0:1, 6:7], wg[0:1, 0:4],
                                                       axis=AXX), s_dv)
                b.run("vector", lambda e: e.reciprocal(wg[0:1, 7:8], wg[0:1, 6:7]),
                      s_dv)
                b.run("vector", lambda e: e.tensor_scalar(
                    wg[0:1, 0:4], wg[0:1, 0:4], wg[0:1, 7:8], None, ALU.mult), s_dv)
                b.run("vector", lambda e: e.tensor_tensor(
                    wg[0:1, 0:4], wg[0:1, 0:4], eU[0:1, :], ALU.mult), s_dv)
                b.run("vector", lambda e, h=h: e.tensor_scalar(
                    wg[0:1, 0:4], wg[0:1, 0:4], g_bc[0:1, h:h + 1], None, ALU.mult),
                    s_dv)
                t3 = b.run("vector", lambda e: e.tensor_copy(wg_bf[:, :],
                                                             wg[0:1, 0:4]), s_dv)
                b.wait("tensor", t3)
                t4 = b.run("tensor", lambda e, wcol=wcol: e.matmul(
                    P[6][:, wcol:wcol + 4], ones1[:, :], wg_bf[:, :], start=True,
                    stop=True), s_pe)
                b.wait("vector", t4)
                t_wbc = b.run("vector", lambda e, wcol=wcol: e.tensor_copy(
                    wbc[:, :], P[6][:, wcol:wcol + 4]), s_dv)
                return dict(h=h, st=st, sl=sl, tse=tse, tds=tds, tmh=tmh,
                            t_wbc=t_wbc, pa_free=None, umm_free=None,
                            pnf=[None, None], last_dve=None,
                            last_tintra=None)

            def emit_B_j(C, j):
                h, st, sl = C["h"], C["st"], C["sl"]
                steff, scb = steffS[st], scbS[st]
                umm, outst = ummS[st], outstS[st]
                dcol, ncol, wbc = dcolS[st], ncolS[st], wbcS[st]
                PA = P[4] if st == 0 else P[3]
                Pn = P[st]
                cj = (j % 2) * 256
                sc = st * 256 + (j % 2) * 128
                nsl = 384 + st * 32 + (j % 2) * 8
                b.wait("tensor", C["pnf"][j % 2])
                if j == 0:
                    b.wait("tensor", C["tds"], C["tmh"], C["tse"])
                tsc = b.run("tensor", lambda e, h=h, j=j, sl=sl, sc=sc: e.matmul(
                    P[2][:, sc:sc + 128],
                    kpds[sl][:, j * 128:(j + 1) * 128],
                    bigB[h][:, j * 128:(j + 1) * 128], start=True, stop=True),
                    s_pe)
                b.run("tensor", lambda e, h=h, j=j, cj=cj, Pn=Pn, steff=steff:
                      e.matmul(
                    Pn[:, cj:cj + 129], bigB[h][:, j * 128:(j + 1) * 128],
                    steff[:, :], start=True, stop=False), s_pe)
                if j > 0:
                    b.run("tensor", lambda e, h=h, j=j, cj=cj, Pn=Pn, sl=sl:
                          e.matmul(
                        Pn[:, cj:cj + 129], bigB[h][:, j * 128:(j + 1) * 128],
                        Lb[sl][:, (j - 1) * 129:j * 129], start=False,
                        stop=False), s_pe)
                b.wait("tensor", C["pa_free"])
                tpa = b.run("tensor", lambda e, h=h, j=j, sl=sl, PA=PA: e.matmul(
                    PA[:, 0:512], bigB[h][:, j * 128:(j + 1) * 128],
                    memh[sl][:, 0:512], start=True, stop=True), s_pe)
                b.run("tensor", lambda e, h=h, j=j, nsl=nsl: e.matmul(
                    P[6][:, nsl:nsl + 4], bigB[h][:, j * 128:(j + 1) * 128],
                    mnT[:, h * 4:h * 4 + 4], start=True, stop=True), s_pe)
                b.wait("vector", tsc)
                tm1 = b.run("vector", lambda e, sc=sc, scb=scb: e.tensor_tensor(
                    scb[:, :], P[2][:, sc:sc + 128], maskT[:, :], ALU.mult),
                    s_dv)
                b.wait("tensor", tm1)
                tintra = b.run("tensor", lambda e, j=j, sl=sl, cj=cj, Pn=Pn,
                               scb=scb: e.matmul(
                    Pn[:, cj:cj + 129], scb[:, :],
                    vaug[sl][:, j * 132:j * 132 + 129], start=False, stop=True),
                    s_pe)
                C["last_tintra"] = tintra
                b.wait("vector", tintra, C["t_wbc"])
                b.run("vector", lambda e, cj=cj, Pn=Pn, dcol=dcol: e.tensor_scalar(
                    dcol[:, 0:1], Pn[:, cj + 128:cj + 129], EPS_ATT, None, ALU.max),
                    s_dv)
                b.run("vector", lambda e, dcol=dcol: e.reciprocal(
                    dcol[:, 1:2], dcol[:, 0:1]), s_dv)
                b.run("vector", lambda e, h=h, dcol=dcol: e.tensor_scalar(
                    dcol[:, 2:3], dcol[:, 1:2], gm1_bc[:, h:h + 1], None,
                    ALU.mult), s_dv)
                b.run("vector", lambda e, nsl=nsl, ncol=ncol: e.tensor_scalar(
                    ncol[:, 0:4], P[6][:, nsl:nsl + 4], EPS_ATT, None, ALU.max),
                    s_dv)
                b.run("vector", lambda e, ncol=ncol: e.reciprocal(
                    ncol[:, 4:8], ncol[:, 0:4]), s_dv)
                tnc = b.run("vector", lambda e, ncol=ncol, wbc=wbc: e.tensor_tensor(
                    ncol[:, 0:4], ncol[:, 4:8], wbc[:, :], ALU.mult), s_dv)
                b.wait("scalar", tpa, tnc, C["umm_free"])
                ta0 = b.run("scalar", lambda e, PA=PA, ncol=ncol, umm=umm:
                            e.activation(umm[:, 0:128], PA[:, 0:128], AF.Copy,
                                         scale=ncol[:, 0:1]), s_ac)
                ta2 = b.run("scalar", lambda e, PA=PA, ncol=ncol, umm=umm:
                            e.activation(umm[:, 128:256], PA[:, 256:384],
                                         AF.Copy, scale=ncol[:, 2:3]), s_ac)
                b.wait("vector", ta0, ta2)
                b.run("vector", lambda e, PA=PA, ncol=ncol, umm=umm:
                      e.scalar_tensor_tensor(
                          umm[:, 256:384], PA[:, 128:256], ncol[:, 1:2],
                          umm[:, 0:128], ALU.mult, ALU.add), s_dv)
                td3 = b.run("vector", lambda e, PA=PA, ncol=ncol, umm=umm:
                            e.scalar_tensor_tensor(
                                umm[:, 384:512], PA[:, 384:512], ncol[:, 3:4],
                                umm[:, 128:256], ALU.mult, ALU.add), s_dv)
                b.run("vector", lambda e, cj=cj, Pn=Pn,
                      dcol=dcol, umm=umm: e.scalar_tensor_tensor(
                          umm[:, 0:128],
                          Pn[:, cj:cj + 128], dcol[:, 2:3], umm[:, 256:384],
                          ALU.mult, ALU.add), s_dv)
                tf2 = b.run("vector", lambda e, j=j, outst=outst, umm=umm:
                            e.tensor_tensor(
                                outst[:, j * 128:(j + 1) * 128],
                                umm[:, 0:128],
                                umm[:, 384:512], ALU.add), s_dv)
                C["pa_free"] = td3
                C["umm_free"] = tf2
                C["pnf"][j % 2] = tf2
                C["last_dve"] = tf2

            def emit_B_tail(C):
                h, st = C["h"], C["st"]
                b.wait("sync", C["last_dve"])
                tsp = b.dma("sync", out_hbm[h][:, :].rearrange(
                    "(n p) e -> p n e", p=128),
                    outstS[st][:, :].rearrange("p (n e) -> p n e", e=128), SY)
                b.wait("sync", tsp)
                txb = b.xbar("sync", bigB[h][:, :], out_hbm[h][:, :], SY)
                C["tail_sp"] = tsp
                return txb

            tccs = [None] * H
            xb_out = [None] * H
            tccs[0] = emit_A(0, None, None)
            tccs[1] = emit_A(1, None, None)
            prev_dve = [None, None]
            prev_sp = [None, None]
            for hp in range(0, H, 2):
                CA = emit_B_hdr(hp, 0, tccs[hp], prev_dve[0], prev_sp[0])
                CB = emit_B_hdr(hp + 1, 1, tccs[hp + 1], prev_dve[1], prev_sp[1])
                for j in range(NCH):
                    emit_B_j(CA, j)
                    emit_B_j(CB, j)
                xb_out[hp] = emit_B_tail(CA)
                xb_out[hp + 1] = emit_B_tail(CB)
                prev_dve = [CA["last_dve"], CB["last_dve"]]
                prev_sp = [CA["tail_sp"], CB["tail_sp"]]
                if hp + 2 < H:
                    tccs[hp + 2] = emit_A(hp + 2, CA["last_tintra"],
                                          CA["last_dve"])
                if hp + 3 < H:
                    tccs[hp + 3] = emit_A(hp + 3, CB["last_tintra"],
                                          CB["last_dve"])

            cut(6)
            barrier()

            xbo = [None] * ND
            for dt in range(ND):
                eng, s = (("sync", SY) if dt % 2 == 0 else ("scalar", SC))
                b.wait(eng, wtok["o"])
                xbo[dt] = b.xbar(eng, bigA[dt][:, :],
                                 wbf["o"][:, dt * 128:(dt + 1) * 128], s)
            prev_ev = [None, None, None, None]
            prev_rd = None
            prev_sp2 = [None, None]
            for tt in range(NT):
                b.wait("scalar", prev_rd)
                ldx1 = b.dma("scalar", xt0[:, :],
                             x1_hbm[tt * 128:(tt + 1) * 128, :], SC)
                tadds = []
                for ec in range(4):
                    b.wait("tensor", prev_ev[ec])
                    for hh in range(H):
                        if tt == 0 and ec == 0:
                            b.wait("tensor", xbo[hh], xb_out[hh])
                        tg = b.run("tensor", lambda e, tt=tt, hh=hh, ec=ec:
                                   e.matmul(
                            P[ec][:, :], bigB[hh][:, tt * 128:(tt + 1) * 128],
                            bigA[hh][:, ec * 512:(ec + 1) * 512],
                            start=(hh == 0), stop=(hh == H - 1)), s_pe)
                    b.wait("vector", tg, ldx1 if ec == 0 else None,
                           state["prev_xt1_free"] if ec == 0 else None)
                    ta = b.run("vector", lambda e, ec=ec: e.tensor_tensor(
                        xt1[:, ec * 512:(ec + 1) * 512],
                        xt0[:, ec * 512:(ec + 1) * 512], P[ec][:, :], ALU.add),
                        s_dv)
                    tadds.append(ta)
                    prev_ev[ec] = ta
                prev_rd = tadds[-1]
                tln = ln_stats(xt1, rstd2, negmr2, tt, 32 + tt, tadds[-1])
                b.wait("vector", tln, prev_sp2[tt % 2])
                tcp = b.run("vector", lambda e, tt=tt: e.tensor_scalar(
                    xlnst[tt % 2][:, :], xt1[:, :], rstd2[:, tt:tt + 1],
                    negmr2[:, tt:tt + 1], ALU.mult, ALU.add), s_dv)
                b.wait("scalar", tcp)
                st1 = b.dma("scalar", xln_tm[2][tt * 128:(tt + 1) * 128, :],
                            xlnst[tt % 2][:, :], SC)
                st2 = b.dma("scalar", x2_hbm[tt * 128:(tt + 1) * 128, :], xt1[:, :],
                            SC)
                if DEBUG:
                    b.dma("scalar", dbg_x2[tt * 128:(tt + 1) * 128, :], xt1[:, :],
                          FIN)
                prev_sp2[tt % 2] = st1
                state["prev_xt1_free"] = st2

            cut(7)
            barrier()

            xb2 = [None] * ND
            for dt in range(ND):
                xb2[dt] = b.xbar("scalar", bigA[dt][:, :],
                                 xln_tm[2][:, dt * 128:(dt + 1) * 128], SC)
            for tt in range(NT):
                for dt in range(ND):
                    if tt == 0:
                        b.wait("tensor", xb2[dt])
                    t_router = b.run("tensor", lambda e, tt=tt, dt=dt: e.matmul(
                        P[tt // 8][:, (tt % 8) * 64:(tt % 8) * 64 + 64],
                        bigA[dt][:, tt * 128:(tt + 1) * 128],
                        rwT[:, dt * 64:(dt + 1) * 64],
                        start=(dt == 0), stop=(dt == ND - 1)), s_pe)

            state["prev_tail_act"] = None
            state["prev_tail_dve"] = None
            prev_ld_reader = None
            for tt in range(NT):
                b.wait("scalar", prev_ld_reader)
                ldx = b.dma("scalar", xt0[:, :],
                            x2_hbm[tt * 128:(tt + 1) * 128, :], SC)
                b.wait("vector", ldx)
                b.wait("scalar", ldx)
                prev_ld_reader = ffn_tail(tt, rstd2, negmr2, tt, True, None, None,
                                          0, None)

            dump_dbg()
            dump_dbg()
            fin_tok = FIN.val()
            b.wait("scalar", fin_tok)
            b.wait("sync", fin_tok)
            b.wait("gpsimd", fin_tok)

        except _Cut:
            pass

        with nc.Block() as block:

            @block.sync
            def _(eng):
                for fn in b.ops["sync"]:
                    fn(eng)

            @block.scalar
            def _(eng):
                for fn in b.ops["scalar"]:
                    fn(eng)

            @block.vector
            def _(eng):
                for fn in b.ops["vector"]:
                    fn(eng)

            @block.tensor
            def _(eng):
                for fn in b.ops["tensor"]:
                    fn(eng)

            @block.gpsimd
            def _(eng):
                for fn in b.ops["gpsimd"]:
                    fn(eng)

    return nc


_NC_CACHE = {}


def kernel(**inputs):
    x = np.ascontiguousarray(np.asarray(inputs["L0_attn_hidden_states"],
                                        dtype=np.float32))
    w_q = np.ascontiguousarray(np.asarray(inputs["L0_attn_w_q"], np.float32))
    w_k = np.ascontiguousarray(np.asarray(inputs["L0_attn_w_k"], np.float32))
    w_v = np.ascontiguousarray(np.asarray(inputs["L0_attn_w_v"], np.float32))
    w_o = np.ascontiguousarray(np.asarray(inputs["L0_attn_w_o"], np.float32))
    memories = np.ascontiguousarray(np.asarray(inputs["L0_attn_memories"],
                                               np.float32))
    mn = np.ascontiguousarray(
        np.asarray(inputs["L0_attn_memory_norms"], np.float32)
        .transpose(1, 0, 2).reshape(H * 4, 128))
    gate = np.asarray(inputs["L0_attn_gate"], np.float32).reshape(1, H)
    r_w = np.ascontiguousarray(np.asarray(inputs["L0_ffn_router_weight"],
                                          np.float32))
    r_b = np.asarray(inputs["L0_ffn_router_bias"], np.float32).reshape(1, 64)
    eb = np.ascontiguousarray(np.asarray(inputs["L0_ffn_expert_bias"],
                                         np.float32))

    identx = np.eye(128, dtype=ml_dtypes.bfloat16)
    maskTx = np.triu(np.ones((128, 128), dtype=np.float32)).astype(
        ml_dtypes.bfloat16)
    mgrpx = np.zeros((64, 4), dtype=ml_dtypes.bfloat16)
    for p in range(64):
        mgrpx[p, p % 4] = 1
    ones1x = np.ones((1, 128), dtype=ml_dtypes.bfloat16)

    if "nc" not in _NC_CACHE:
        _NC_CACHE["nc"] = build_program()
    nc = _NC_CACHE["nc"]

    in_maps = []
    for c in range(8):
        bi, hf = c // 2, c % 2
        sf = np.full((128, 1), 1.0 if hf == 0 else 0.0, np.float32)
        in_maps.append(dict(
            x0=np.ascontiguousarray(x[bi, hf * T:(hf + 1) * T, :]),
            w_q=w_q, w_k=w_k, w_v=w_v, w_o=w_o,
            mems=memories, mn=mn, gate=gate, r_w=r_w, r_b=r_b, eb=eb,
            identc=identx, maskTc=maskTx, mgrpc=mgrpx, ones1c=ones1x,
            sflagc=sf, rflagc=1.0 - sf,
        ))
    res = run_bass_kernel_spmd(nc, in_maps, core_ids=list(range(8)))
    out = np.empty((4, 4096, D), np.float32)
    for c in range(8):
        bi, hf = c // 2, c % 2
        out[bi, hf * T:(hf + 1) * T, :] = res.results[c]["y"]
    if DEBUG:
        kernel.dbg = res.results
    return out


if __name__ == "__main__":
    nc = build_program()
    print("build ok; instructions:", nc.n_instructions()
          if hasattr(nc, "n_instructions") else "?")
